# version 01
# baseline (speedup 1.0000x reference)
import numpy as np
from contextlib import ExitStack

import jax

try:
    jax.config.update("jax_compilation_cache_dir", "/tmp/jax_kernel_cache")
    jax.config.update("jax_persistent_cache_min_compile_time_secs", 1.0)
except Exception:
    pass

import concourse.bass as bass
import concourse.mybir as mybir
from concourse.bass_utils import run_bass_kernel_spmd

F32 = mybir.dt.float32
BF16 = mybir.dt.bfloat16
ALU = mybir.AluOpType
AF = mybir.ActivationFunctionType
AX = mybir.AxisListType

MAGIC = 12582912.0
LN_EPS = 1e-5

B = 2
S = 8192
D = 1024
E = 2048
N = 16
I3 = 3072
TL = 2048
HALO = 4
DC0 = 256
DC1 = 512
TC = 512

ENGINES = ("tensor", "vector", "scalar", "gpsimd", "sync")


class _Op:
    __slots__ = ("eng", "fn", "deps", "is_dma", "need_inc", "count_after", "tag")

    def __init__(self, eng, fn, deps, is_dma, tag):
        self.eng = eng
        self.fn = fn
        self.deps = deps
        self.is_dma = is_dma
        self.need_inc = False
        self.count_after = None
        self.tag = tag


class Prog:

    def __init__(self, nc):
        self.nc = nc
        self.ops = {e: [] for e in ENGINES}

    def add(self, eng, fn, deps=(), dma=False, tag=""):
        op = _Op(eng, fn, tuple(d for d in deps if d is not None), dma, tag)
        self.ops[eng].append(op)
        return op

    def dma(self, eng, out, in_, deps=(), tag="", transpose=False):
        if transpose:
            fn = lambda e, out=out, in_=in_: e.dma_start(out=out, in_=in_, transpose=True)
        else:
            fn = lambda e, out=out, in_=in_: e.dma_start(out=out, in_=in_)
        return self.add(eng, fn, deps, dma=True, tag=tag)

    def emit(self, es, block):
        dma_index = {}
        for e in ENGINES:
            i = 0
            for op in self.ops[e]:
                if op.is_dma:
                    dma_index[id(op)] = i
                    i += 1
        for e in ENGINES:
            for op in self.ops[e]:
                if op.is_dma:
                    op.need_inc = True
                per_q = {}
                for d in op.deps:
                    if d.is_dma:
                        cur = per_q.get(d.eng)
                        if cur is None or dma_index[id(d)] > dma_index[id(cur)]:
                            per_q[d.eng] = d
                    else:
                        d.need_inc = True
                for d in per_q.values():
                    d.need_inc = True
                    d.tag = d.tag + "|ckpt" if "|ckpt" not in d.tag else d.tag
        sems = {}
        dsems = {}
        for e in ENGINES:
            comp = [o for o in self.ops[e] if not o.is_dma]
            dmas = [o for o in self.ops[e] if o.is_dma]
            if comp:
                comp[-1].need_inc = True
                sems[e] = es.enter_context(self.nc.semaphore(f"sem_{e}"))
            if dmas:
                dsems[e] = es.enter_context(self.nc.semaphore(f"dsem_{e}"))
        totals = {}
        dtotals = {}
        for e in ENGINES:
            cc = cd = 0
            for op in self.ops[e]:
                if op.is_dma:
                    cd += 16
                    op.count_after = cd
                else:
                    if op.need_inc:
                        cc += 1
                    op.count_after = cc
            totals[e] = cc
            dtotals[e] = cd
        for e in ENGINES:
            if not self.ops[e]:
                continue

            def _section(engine, e=e):
                water = {(x, c): 0 for x in ENGINES for c in (0, 1)}
                pending_ckpt = 0
                for op in self.ops[e]:
                    if op.is_dma and pending_ckpt:
                        if pending_ckpt > water[(e, 1)]:
                            engine.wait_ge(dsems[e], pending_ckpt)
                            water[(e, 1)] = pending_ckpt
                        pending_ckpt = 0

                    want = {}
                    for d in op.deps:
                        tgt = d.count_after
                        if tgt == 0:
                            raise RuntimeError(f"dep without inc: {d.tag} <- {op.tag}")
                        key = (d.eng, 1 if d.is_dma else 0)
                        if tgt > want.get(key, 0):
                            want[key] = tgt
                    for key, tgt in want.items():
                        if tgt > water[key]:
                            engine.wait_ge(dsems[key[0]] if key[1] else sems[key[0]], tgt)
                            water[key] = tgt
                    instr = op.fn(engine)
                    if op.need_inc:
                        if instr is None:
                            raise RuntimeError(f"op {op.tag} returned no instruction")
                        instr.then_inc(dsems[e] if op.is_dma else sems[e],
                                       16 if op.is_dma else 1)
                        if op.is_dma and "|ckpt" in op.tag:
                            pending_ckpt = op.count_after

            getattr(block, e)(_section)
        fin_eng = "sync" if self.ops["sync"] else "gpsimd"

        def _fin(engine):
            for e in ENGINES:
                if e in sems and totals[e] and e != fin_eng:
                    engine.wait_ge(sems[e], totals[e])
                if e in dsems and dtotals[e]:
                    engine.wait_ge(dsems[e], dtotals[e])
        getattr(block, fin_eng)(_fin)


def _alloc(es, nc, name, shape, dtype):
    return es.enter_context(nc.sbuf_tensor(name, shape, dtype))


def _palloc(es, nc, name, shape, dtype=F32):
    return es.enter_context(nc.psum_tensor(name, shape, dtype))



def scan_chunk(p, bufs, DT, dep_delta, dep_v, dep_bcal, dep_state_prev,
               prev_chunk_readers, tag):
    (acol, state, eye_b, sel_sb, bcal, deltaT, vT,
     ps_bb, ps_cb, ps_y, bbc, cbc, decay, btile, htile, cm) = bufs
    nbc = len(ps_bb)

    n_idx = lambda n_i, dt_i: n_i * DT + dt_i
    bb_mm = [None] * N
    cb_mm = [None] * N
    bb_ev = [None] * N
    cb_ev = [None] * N
    dec_ops = [None] * (N * DT)
    b_ops = [None] * (N * DT)
    scan_ops = [None] * (N * DT)
    st_ops = [None] * (N * DT)
    cm_ops = [None] * (N * DT)
    y_ops = [None] * (N * DT)
    pr = prev_chunk_readers

    for n_i in range(N):
        bp = n_i % 2
        pp = n_i % nbc
        war_b = [bb_ev[n_i - nbc]] if n_i >= nbc else pr.get("ps_bb_%d" % pp, [])
        bb_mm[n_i] = p.add("tensor", lambda t, n_i=n_i, pp=pp: t.matmul(
            ps_bb[pp][:], sel_sb[:, n_i * 128:(n_i + 1) * 128], bcal,
            start=True, stop=True),
            deps=[dep_bcal] + war_b, tag=f"{tag}bbmm{n_i}")
        war_c = [cb_ev[n_i - nbc]] if n_i >= nbc else pr.get("ps_cb_%d" % pp, [])
        cb_mm[n_i] = p.add("tensor", lambda t, n_i=n_i, pp=pp: t.matmul(
            ps_cb[pp][:], sel_sb[:, (N + n_i) * 128:(N + n_i + 1) * 128], bcal,
            start=True, stop=True),
            deps=[dep_bcal] + war_c, tag=f"{tag}cbmm{n_i}")
        war_bbc = [b_ops[n_idx(n_i - 2, DT - 1)]] if n_i >= 2 else pr.get("bbc_%d" % bp, [])
        bb_ev[n_i] = p.add("scalar", lambda s, bp=bp, pp=pp: s.copy(bbc[bp][:], ps_bb[pp][:]),
                           deps=[bb_mm[n_i]] + war_bbc, tag=f"{tag}bbev{n_i}")
        war_cbc = [cm_ops[n_idx(n_i - 2, DT - 1)]] if n_i >= 2 else pr.get("cbc_%d" % bp, [])
        cb_ev[n_i] = p.add("scalar", lambda s, bp=bp, pp=pp: s.copy(cbc[bp][:], ps_cb[pp][:]),
                           deps=[cb_mm[n_i]] + war_cbc, tag=f"{tag}cbev{n_i}")
        for dt_i in range(DT):
            idx = n_idx(n_i, dt_i)
            ip = idx % 2
            war_dec = [scan_ops[idx - 2]] if idx >= 2 else pr.get("decay_%d" % ip, [])
            dec_ops[idx] = p.add("scalar", lambda s, dt_i=dt_i, n_i=n_i, ip=ip: s.activation(
                decay[ip][:], deltaT[:, dt_i * TC:(dt_i + 1) * TC], AF.Exp,
                scale=acol[:, dt_i * N + n_i: dt_i * N + n_i + 1]),
                deps=[dep_delta] + war_dec, tag=f"{tag}dec{idx}")
            war_bt = [scan_ops[idx - 2]] if idx >= 2 else pr.get("btile_%d" % ip, [])
            b_ops[idx] = p.add("vector", lambda v, dt_i=dt_i, ip=ip, bp=bp: v.tensor_mul(
                btile[ip][:], vT[:, dt_i * TC:(dt_i + 1) * TC], bbc[bp][:]),
                deps=[dep_v, bb_ev[n_i]] + war_bt, tag=f"{tag}b{idx}")
            war_h = ([cm_ops[idx - 2], st_ops[idx - 2]] if idx >= 2
                     else pr.get("htile_%d" % ip, []))
            col = dt_i * N + n_i
            scan_ops[idx] = p.add("vector", lambda v, ip=ip, col=col: v.tensor_tensor_scan(
                htile[ip][:], decay[ip][:], btile[ip][:], state[:, col:col + 1],
                op0=ALU.mult, op1=ALU.add),
                deps=[dec_ops[idx], b_ops[idx], dep_state_prev[col]] + war_h,
                tag=f"{tag}scan{idx}")
            st_ops[idx] = p.add("scalar", lambda s, ip=ip, col=col: s.copy(
                state[:, col:col + 1], htile[ip][:, TC - 1:TC]),
                deps=[scan_ops[idx]], tag=f"{tag}st{idx}")
            war_cm = [y_ops[idx - 2]] if idx >= 2 else pr.get("cm_%d" % ip, [])
            cm_ops[idx] = p.add("vector", lambda v, ip=ip, bp=bp: v.tensor_mul(
                cm[ip][:], htile[ip][:], cbc[bp][:]),
                deps=[scan_ops[idx], st_ops[idx], cb_ev[n_i]] + war_cm,
                tag=f"{tag}cm{idx}")
            war_y = pr.get("ps_y_%d" % dt_i, []) if n_i == 0 else []
            y_ops[idx] = p.add("tensor", lambda t, ip=ip, dt_i=dt_i, n_i=n_i: t.matmul(
                ps_y[dt_i][:], eye_b[:], cm[ip][:],
                start=(n_i == 0), stop=(n_i == N - 1)),
                deps=[cm_ops[idx]] + war_y, tag=f"{tag}y{idx}")

    readers = {}
    for bpv in range(nbc):
        readers["ps_bb_%d" % bpv] = [bb_ev[N - nbc + bpv]]
        readers["ps_cb_%d" % bpv] = [cb_ev[N - nbc + bpv]]
    for bpv in range(2):
        readers["bbc_%d" % bpv] = [b_ops[n_idx(N - 2 + bpv, DT - 1)]]
        readers["cbc_%d" % bpv] = [cm_ops[n_idx(N - 2 + bpv, DT - 1)]]
    for ipv in range(2):
        last = N * DT - 2 + ipv
        readers["decay_%d" % ipv] = [scan_ops[last]]
        readers["btile_%d" % ipv] = [scan_ops[last]]
        readers["htile_%d" % ipv] = [cm_ops[last], st_ops[last]]
        readers["cm_%d" % ipv] = [y_ops[last]]
    state_ops = {}
    for n_i in range(N):
        for dt_i in range(DT):
            state_ops[dt_i * N + n_i] = st_ops[n_idx(n_i, dt_i)]
    ylast = [y_ops[n_idx(N - 1, dt_i)] for dt_i in range(DT)]
    readers["_bcal"] = [bb_mm[N - 1], cb_mm[N - 1]]
    readers["_delta"] = [dec_ops[N * DT - 1]]
    readers["_v"] = [b_ops[N * DT - 1]]
    return ylast, readers, state_ops



def build_phase1():
    nc = bass.Bass("TRN2", target_bir_lowering=False, detect_race_conditions=False)
    x = nc.dram_tensor("x", [S, D], F32, kind="ExternalInput").ap()
    WdT = nc.dram_tensor("WdT", [D, DC0], F32, kind="ExternalInput").ap()
    WBC = nc.dram_tensor("WBC", [D, 2 * N], F32, kind="ExternalInput").ap()
    Alog = nc.dram_tensor("Alog", [DC0, N], F32, kind="ExternalInput").ap()
    eye = nc.dram_tensor("eye", [128, 128], F32, kind="ExternalInput").ap()
    sel = nc.dram_tensor("sel", [2 * N, 2 * N * 128], F32, kind="ExternalInput").ap()
    y0 = nc.dram_tensor("y0", [DC0, S], F32, kind="ExternalOutput").ap()

    n_chunks = S // TC
    KT = D // 128
    DT = DC0 // 128

    es = ExitStack()
    with es:
        wd_sb = _alloc(es, nc, "wd_sb", [128, KT * DC0], BF16)
        wbc_sb = _alloc(es, nc, "wbc_sb", [128, KT * 2 * N], BF16)
        acol = _alloc(es, nc, "acol", [128, DT * N], F32)
        eye_b = _alloc(es, nc, "eye_b", [128, 128], BF16)
        sel_sb = _alloc(es, nc, "sel_sb", [2 * N, 2 * N * 128], BF16)
        xtm = [_alloc(es, nc, f"xtm{i}", [128, D], BF16) for i in range(8)]
        xfm = [_alloc(es, nc, f"xfm{i}", [128, KT * TC], BF16) for i in range(2)]
        deltaT = [_alloc(es, nc, f"deltaT{i}", [128, DT * TC], F32) for i in range(2)]
        sp_tmp = _alloc(es, nc, "sp_tmp", [128, TC], F32)
        vT = [_alloc(es, nc, f"vT{i}", [128, DT * TC], BF16) for i in range(2)]
        bcal = [_alloc(es, nc, f"bcal{i}", [2 * N, TC], BF16) for i in range(2)]
        bbc = [_alloc(es, nc, f"bbc{i}", [128, TC], BF16) for i in range(2)]
        cbc = [_alloc(es, nc, f"cbc{i}", [128, TC], BF16) for i in range(2)]
        decay = [_alloc(es, nc, f"decay{i}", [128, TC], F32) for i in range(2)]
        btile = [_alloc(es, nc, f"btile{i}", [128, TC], BF16) for i in range(2)]
        htile = [_alloc(es, nc, f"htile{i}", [128, TC], F32) for i in range(2)]
        cm = [_alloc(es, nc, f"cm{i}", [128, TC], BF16) for i in range(2)]
        state = _alloc(es, nc, "state", [128, DT * N], F32)
        ytile = [_alloc(es, nc, f"ytile{i}", [128, DT * TC], F32) for i in range(2)]
        ps_mm = _palloc(es, nc, "ps_mm", [128, TC])
        ps_bcal = _palloc(es, nc, "ps_bcal", [2 * N, TC])
        ps_bb = [_palloc(es, nc, f"ps_bb{i}", [128, TC]) for i in range(2)]
        ps_cb = [_palloc(es, nc, f"ps_cb{i}", [128, TC]) for i in range(2)]
        ps_y = [_palloc(es, nc, f"ps_y{i}", [128, TC]) for i in range(2)]

        block = es.enter_context(nc.Block())
        p = Prog(nc)

        w_loads = []
        for k in range(KT):
            w_loads.append(p.dma("gpsimd", wd_sb[:, k * DC0:(k + 1) * DC0],
                                 WdT[k * 128:(k + 1) * 128, :], tag=f"wd{k}"))
            w_loads.append(p.dma("gpsimd", wbc_sb[:, k * 2 * N:(k + 1) * 2 * N],
                                 WBC[k * 128:(k + 1) * 128, :], tag=f"wbc{k}"))
        w_loads.append(p.dma("gpsimd", eye_b[:], eye, tag="eye"))
        w_loads.append(p.dma("gpsimd", sel_sb[:], sel, tag="sel"))
        alog_ld = p.dma("gpsimd",
                        acol[:].rearrange("p (dt n) -> p dt n", dt=DT),
                        Alog.rearrange("(dt p) n -> p dt n", dt=DT), tag="alog")
        st0 = p.add("gpsimd", lambda g: g.memset(state[:], 0.0), tag="st0")
        a_exp = p.add("scalar", lambda s: s.activation(acol[:], acol[:], AF.Exp),
                      deps=[alog_ld], tag="aexp")
        a_neg = p.add("vector", lambda v: v.tensor_scalar_mul(acol[:], acol[:], -1.0),
                      deps=[a_exp], tag="aneg")

        prev = {}
        state_prev = {c: st0 for c in range(DT * N)}
        xload_ops = [[None] * 4 for _ in range(n_chunks)]
        tp_ops = [[None] * (4 * KT) for _ in range(n_chunks)]

        for ci in range(n_chunks):
            par = ci % 2
            t0 = ci * TC
            tg = f"c{ci}_"

            for st_i in range(4):
                war = tp_ops[ci - 2][st_i * KT:(st_i + 1) * KT] if ci >= 2 else []
                xload_ops[ci][st_i] = p.dma(
                    "gpsimd", xtm[par * 4 + st_i][:],
                    x[t0 + st_i * 128: t0 + (st_i + 1) * 128, :],
                    deps=war, tag=f"{tg}xl{st_i}")
            war_xfm = prev.get("xfm_%d" % par, [])
            for st_i in range(4):
                for k in range(KT):
                    tp_ops[ci][st_i * KT + k] = p.dma(
                        "sync",
                        xfm[par][:, k * TC + st_i * 128: k * TC + (st_i + 1) * 128],
                        xtm[par * 4 + st_i][:, k * 128:(k + 1) * 128],
                        deps=[xload_ops[ci][st_i]] + war_xfm,
                        tag=f"{tg}tp{st_i}_{k}", transpose=True)
            tps = tp_ops[ci]

            d_ev = [None] * DT
            sp_war = prev.get("sp_tmp", [])
            for dt_i in range(DT):
                war_psmm = [d_ev[dt_i - 1]] if dt_i > 0 else prev.get("ps_mm", [])
                for k in range(KT):
                    mm = p.add("tensor", lambda t, k=k, dt_i=dt_i, par=par: t.matmul(
                        ps_mm[:], wd_sb[:, k * DC0 + dt_i * 128: k * DC0 + (dt_i + 1) * 128],
                        xfm[par][:, k * TC:(k + 1) * TC],
                        start=(k == 0), stop=(k == KT - 1)),
                        deps=(w_loads + tps + war_psmm) if k == 0 else [],
                        tag=f"{tg}dmm{dt_i}_{k}")
                ex = p.add("scalar", lambda s: s.activation(sp_tmp[:], ps_mm[:], AF.Exp),
                           deps=[mm] + sp_war, tag=f"{tg}dex{dt_i}")
                sp_war = []
                ln = p.add("scalar", lambda s, dt_i=dt_i, par=par: s.activation(
                    deltaT[par][:, dt_i * TC:(dt_i + 1) * TC], sp_tmp[:], AF.Ln, bias=1.0),
                    deps=[ex] + prev.get("deltaT_%d" % par, []), tag=f"{tg}dln{dt_i}")
                d_ev[dt_i] = ln
            for k in range(KT):
                mm = p.add("tensor", lambda t, k=k, par=par: t.matmul(
                    ps_bcal[:], wbc_sb[:, k * 2 * N:(k + 1) * 2 * N],
                    xfm[par][:, k * TC:(k + 1) * TC],
                    start=(k == 0), stop=(k == KT - 1)),
                    deps=prev.get("ps_bcal", []) if k == 0 else [],
                    tag=f"{tg}bcmm{k}")
            bc_ev = p.add("scalar", lambda s, par=par: s.copy(bcal[par][:], ps_bcal[:]),
                          deps=[mm] + prev.get("bcal_%d" % par, []), tag=f"{tg}bcev")

            v_ops = []
            for dt_i in range(DT):
                v_ops.append(p.add("vector", lambda v, dt_i=dt_i, par=par: v.tensor_mul(
                    vT[par][:, dt_i * TC:(dt_i + 1) * TC],
                    deltaT[par][:, dt_i * TC:(dt_i + 1) * TC],
                    xfm[par][:, dt_i * TC:(dt_i + 1) * TC]),
                    deps=[d_ev[dt_i]] + prev.get("vT_%d" % par, []),
                    tag=f"{tg}v{dt_i}"))

            dep_state = dict(state_prev)
            bufs = (acol, state, eye_b, sel_sb, bcal[par][:], deltaT[par][:], vT[par][:],
                    ps_bb, ps_cb, ps_y, bbc, cbc, decay, btile, htile, cm)
            dep_delta = d_ev[DT - 1]
            dep_vv = v_ops[-1]
            ylast, readers, state_ops = scan_chunk(
                p, bufs, DT, dep_delta, dep_vv, bc_ev, dep_state, prev, tg)
            state_prev = state_ops

            y_evs = []
            for dt_i in range(DT):
                yev = p.add("scalar", lambda s, dt_i=dt_i, par=par: s.copy(
                    ytile[par][:, dt_i * TC:(dt_i + 1) * TC], ps_y[dt_i][:]),
                    deps=[ylast[dt_i]] + prev.get("ytile_%d" % par, []),
                    tag=f"{tg}yev{dt_i}")
                y_evs.append(yev)
            st_dma = p.dma("scalar",
                           y0.rearrange("(dt p) s -> p dt s", dt=DT)[:, :, t0:t0 + TC],
                           ytile[par][:].rearrange("p (dt t) -> p dt t", dt=DT),
                           deps=y_evs, tag=f"{tg}store")

            nxt = dict(prev)
            nxt.update(readers)
            nxt["xfm_%d" % par] = [mm] + v_ops
            nxt["ps_mm"] = [d_ev[DT - 1]]
            nxt["sp_tmp"] = [d_ev[DT - 1]]
            nxt["deltaT_%d" % par] = v_ops + readers["_delta"]
            nxt["ps_bcal"] = [bc_ev]
            nxt["bcal_%d" % par] = readers["_bcal"]
            nxt["vT_%d" % par] = readers["_v"]
            nxt["ytile_%d" % par] = [st_dma]
            for dt_i in range(DT):
                nxt["ps_y_%d" % dt_i] = [y_evs[dt_i]]
            prev = nxt

        p.emit(es, block)
    return nc


_P1 = None


def _phase1_nc():
    global _P1
    if _P1 is None:
        _P1 = build_phase1()
    return _P1


def run_phase1(inputs):
    x = np.ascontiguousarray(inputs["L0_attn_x"])
    Wd = np.ascontiguousarray(inputs["L0_attn_W_delta"])
    WB = inputs["L0_attn_W_B"]
    WC = inputs["L0_attn_W_C"]
    WBC = np.ascontiguousarray(np.concatenate([WB, WC], axis=1))
    Alog = inputs["L0_attn_A_log"]
    eye = np.eye(128, dtype=np.float32)
    sel = np.zeros((2 * N, 2 * N * 128), dtype=np.float32)
    for i in range(2 * N):
        sel[i, i * 128:(i + 1) * 128] = 1.0
    in_maps = []
    for c in range(8):
        b, j = divmod(c, 4)
        sl = slice(DC0 * j, DC0 * (j + 1))
        order = np.concatenate([np.arange(DC0 * j, DC0 * (j + 1)),
                                np.arange(0, DC0 * j),
                                np.arange(DC0 * (j + 1), D)])
        in_maps.append({
            "x": np.ascontiguousarray(x[b][:, order]),
            "WdT": np.ascontiguousarray(Wd[order][:, sl]),
            "WBC": np.ascontiguousarray(WBC[order]),
            "Alog": np.ascontiguousarray(Alog[sl]),
            "eye": eye,
            "sel": sel,
        })
    res = run_bass_kernel_spmd(_phase1_nc(), in_maps, core_ids=list(range(8)))
    y0 = np.empty((B, S, D), np.float32)
    for c in range(8):
        b, j = divmod(c, 4)
        y0[b][:, DC0 * j:DC0 * (j + 1)] = res.results[c]["y0"].T
    return y0


def kernel(**inputs):
    raise NotImplementedError("phases under construction")



def _mm_chain(p, ps, pairs, deps_first, tag):
    last = None
    nk = len(pairs)
    for k, (lt, rh) in enumerate(pairs):
        last = p.add("tensor", lambda t, lt=lt, rh=rh, k=k, nk=nk: t.matmul(
            ps, lt, rh, start=(k == 0), stop=(k == nk - 1)),
            deps=deps_first if k == 0 else (), tag=f"{tag}_k{k}")
    return last


class LNCtx:

    def __init__(self, es, nc, W):
        self.ps_s1 = _palloc(es, nc, "ln_ps_s1", [1, W])
        self.ps_s2 = _palloc(es, nc, "ln_ps_s2", [1, W])
        self.ps_mb = _palloc(es, nc, "ln_ps_mb", [128, W])
        self.ps_rb = _palloc(es, nc, "ln_ps_rb", [128, W])
        self.sqs = _alloc(es, nc, "ln_sqs", [128, W], F32)
        self.rows = _alloc(es, nc, "ln_rows", [1, 4 * W], F32)
        self.onesc = _alloc(es, nc, "ln_onesc", [128, 1], F32)
        self.eps_c = _alloc(es, nc, "ln_eps", [1, 1], F32)
        self.ones1f = _alloc(es, nc, "ln_ones1f", [1, 128], F32)
        self.prev_readers = []

    def init_ops(self, p):
        o1 = p.add("gpsimd", lambda g: g.memset(self.onesc[:], 1.0), tag="ln_c1")
        o2 = p.add("gpsimd", lambda g: g.memset(self.ones1f[:], 1.0), tag="ln_c2")
        o3 = p.add("gpsimd", lambda g: g.memset(self.eps_c[:], LN_EPS), tag="ln_c3")
        self.prev_readers = [o1, o2, o3]
        self.const_ops = [o1, o2, o3]


def emit_ln(p, L, src, KT, W, dep_src, tag, nfeat,
            gcol=None, bcol=None, out_f32=None, out_b16=None):
    if not isinstance(dep_src, (list, tuple)):
        dep_src = [dep_src]
    """LayerNorm over the partition-tiled feature dim (KT tiles of 128).

    src: sbuf AP [128, KT*W] f32 (k-major slices). Writes out_f32 and/or
    out_b16 ([128, KT*W]). Returns (last_ops_list, src_reader_ops)."""
    ops_mm = []
    guard = L.prev_readers
    sq_prev = None
    mm2 = mm1 = None
    for k in range(KT):
        sq = p.add("scalar", lambda s, k=k: s.activation(
            L.sqs[:, 0:W], src[:, k * W:(k + 1) * W], AF.Square),
            deps=list(dep_src) + ([sq_prev] if sq_prev else guard) + L.const_ops,
            tag=f"{tag}sq{k}")
        mm1 = p.add("tensor", lambda t, k=k: t.matmul(
            L.ps_s1[:, 0:W], L.onesc[:], src[:, k * W:(k + 1) * W],
            start=(k == 0), stop=(k == KT - 1)),
            deps=list(dep_src) + (guard if k == 0 else []), tag=f"{tag}s1_{k}")
        mm2 = p.add("tensor", lambda t, k=k: t.matmul(
            L.ps_s2[:, 0:W], L.onesc[:], L.sqs[:, 0:W],
            start=(k == 0), stop=(k == KT - 1)),
            deps=[sq], tag=f"{tag}s2_{k}")
        sq_prev = mm2
    inv = 1.0 / nfeat
    r_mean = L.rows[:, 0 * W:0 * W + W]
    r_ex2 = L.rows[:, 1 * W:1 * W + W]
    r_var = L.rows[:, 2 * W:2 * W + W]
    r_rstd = L.rows[:, 3 * W:3 * W + W]
    s_mean = p.add("vector", lambda v: v.tensor_scalar_mul(r_mean, L.ps_s1[:, 0:W], inv),
                   deps=[mm1] + guard, tag=f"{tag}mean")
    s_ex2 = p.add("vector", lambda v: v.tensor_scalar_mul(r_ex2, L.ps_s2[:, 0:W], inv),
                  deps=[mm2] + guard, tag=f"{tag}ex2")
    s_var = p.add("vector", lambda v: v.scalar_tensor_tensor(
        r_var, r_mean, -1.0, r_mean, op0=ALU.mult, op1=ALU.mult),
        deps=[s_mean], tag=f"{tag}m2")
    s_var2 = p.add("vector", lambda v: v.tensor_add(r_var, r_var, r_ex2),
                   deps=[s_var, s_ex2], tag=f"{tag}var")
    s_sd = p.add("scalar", lambda s: s.activation(r_rstd, r_var, AF.Sqrt,
                                                  bias=L.eps_c[0:1, 0:1]),
                 deps=[s_var2] + L.const_ops, tag=f"{tag}sd")
    s_rstd = p.add("vector", lambda v: v.reciprocal(r_rstd, r_rstd),
                   deps=[s_sd], tag=f"{tag}rstd")
    mb = p.add("tensor", lambda t: t.matmul(L.ps_mb[:, 0:W], L.ones1f[:], r_mean,
                                            start=True, stop=True),
               deps=[s_mean] + guard, tag=f"{tag}mb")
    rb = p.add("tensor", lambda t: t.matmul(L.ps_rb[:, 0:W], L.ones1f[:], r_rstd,
                                            start=True, stop=True),
               deps=[s_rstd] + guard, tag=f"{tag}rb")
    outs = []
    readers = [s_mean, s_ex2, mm2]
    tmp = L.sqs
    for k in range(KT):
        t1 = p.add("vector", lambda v, k=k: v.tensor_sub(
            tmp[:, 0:W], src[:, k * W:(k + 1) * W], L.ps_mb[:, 0:W]),
            deps=[mb, sq_prev] if k == 0 else [mb], tag=f"{tag}a1_{k}")
        if gcol is not None:
            t2 = p.add("vector", lambda v: v.tensor_mul(tmp[:, 0:W], tmp[:, 0:W], L.ps_rb[:, 0:W]),
                       deps=[t1, rb], tag=f"{tag}a2_{k}")
            o = None
            if out_f32 is not None:
                o = p.add("vector", lambda v, k=k: v.tensor_scalar(
                    out_f32[:, k * W:(k + 1) * W], tmp[:, 0:W],
                    gcol[:, k:k + 1], bcol[:, k:k + 1],
                    op0=ALU.mult, op1=ALU.add),
                    deps=[t2], tag=f"{tag}af{k}")
                outs.append(o)
            if out_b16 is not None:
                o = p.add("vector", lambda v, k=k: v.tensor_scalar(
                    out_b16[:, k * W:(k + 1) * W], tmp[:, 0:W],
                    gcol[:, k:k + 1], bcol[:, k:k + 1],
                    op0=ALU.mult, op1=ALU.add),
                    deps=[t2], tag=f"{tag}ab{k}")
                outs.append(o)
            readers.append(t2)
            readers.append(o)
        else:
            lastw = None
            if out_f32 is not None:
                lastw = p.add("vector", lambda v, k=k: v.tensor_mul(
                    out_f32[:, k * W:(k + 1) * W], tmp[:, 0:W], L.ps_rb[:, 0:W]),
                    deps=[t1, rb], tag=f"{tag}a2f{k}")
                outs.append(lastw)
            if out_b16 is not None:
                lastw = p.add("vector", lambda v, k=k: v.tensor_mul(
                    out_b16[:, k * W:(k + 1) * W], tmp[:, 0:W], L.ps_rb[:, 0:W]),
                    deps=[t1, rb], tag=f"{tag}a2b{k}")
                outs.append(lastw)
            readers.append(lastw)
    L.prev_readers = readers + outs
    return outs


P2_STAGES = 9


def build_phase2():
    stages = P2_STAGES
    nc = bass.Bass("TRN2", target_bir_lowering=False, detect_race_conditions=False)
    WL = HALO + TL
    widths = [HALO] + [512] * 4
    c0s = [0, 4, 516, 1028, 1540]
    KT = 8
    ET = 16

    y0_tok = nc.dram_tensor("y0_tok", [D, WL], F32, kind="ExternalInput").ap()
    x_tok = nc.dram_tensor("x_tok", [WL, D], F32, kind="ExternalInput").ap()
    lng = nc.dram_tensor("lng", [128, KT], F32, kind="ExternalInput").ap()
    lnb = nc.dram_tensor("lnb", [128, KT], F32, kind="ExternalInput").ap()
    csc = nc.dram_tensor("csc", [128, 1], F32, kind="ExternalInput").ap()
    mur = nc.dram_tensor("mur", [128, KT], F32, kind="ExternalInput").ap()
    muk = nc.dram_tensor("muk", [128, KT], F32, kind="ExternalInput").ap()
    bin_ = nc.dram_tensor("bin", [128, 32], F32, kind="ExternalInput").ap()
    convw = nc.dram_tensor("convw", [128, ET * 4], F32, kind="ExternalInput").ap()
    convb = nc.dram_tensor("convb", [128, ET], F32, kind="ExternalInput").ap()
    eyef = nc.dram_tensor("eyef", [128, 128], F32, kind="ExternalInput").ap()
    hmask = nc.dram_tensor("hmask", [128, 32], F32, kind="ExternalInput").ap()
    Wctx = nc.dram_tensor("Wctx", [D, D], F32, kind="ExternalInput").ap()
    WrT = nc.dram_tensor("WrT", [D, D], F32, kind="ExternalInput").ap()
    WkT = nc.dram_tensor("WkT", [D, D], F32, kind="ExternalInput").ap()
    WvT = nc.dram_tensor("WvT", [D, D], F32, kind="ExternalInput").ap()
    Win = nc.dram_tensor("Win", [D, 2 * E], F32, kind="ExternalInput").ap()
    xpw = nc.dram_tensor("xpw", [E, 160], F32, kind="ExternalInput").ap()

    s2_out = nc.dram_tensor("s2", [D, TL], F32, kind="ExternalOutput").ap()
    xr_out = nc.dram_tensor("xr", [E, TL], F32, kind="ExternalOutput").ap()
    xc_out = nc.dram_tensor("xc", [E, TL], F32, kind="ExternalOutput").ap()
    dt_out = nc.dram_tensor("dt", [128, TL], F32, kind="ExternalOutput").ap()
    bc_out = nc.dram_tensor("bc", [32, TL], F32, kind="ExternalOutput").ap()

    Wmax = 512
    es = ExitStack()
    with es:
        lng_s = _alloc(es, nc, "lng_s", [128, KT], F32)
        lnb_s = _alloc(es, nc, "lnb_s", [128, KT], F32)
        csc_s = _alloc(es, nc, "csc_s", [128, 1], F32)
        mur_s = _alloc(es, nc, "mur_s", [128, KT], F32)
        muk_s = _alloc(es, nc, "muk_s", [128, KT], F32)
        bin_s = _alloc(es, nc, "bin_s", [128, 32], F32)
        cw_s = _alloc(es, nc, "cw_s", [128, ET * 4], F32)
        cb_s = _alloc(es, nc, "cb_s", [128, ET], F32)
        eyef_s = _alloc(es, nc, "eyef_s", [128, 128], F32)
        hmask_s = _alloc(es, nc, "hmask_s", [128, 32], F32)
        wu = [_alloc(es, nc, f"wu{i}", [128, KT * D], BF16) for i in range(2)]
        y0t = _alloc(es, nc, "y0t", [128, KT * Wmax], F32)
        x0st = [_alloc(es, nc, f"x0st{i}", [128, D], F32) for i in range(2)]
        x0T = _alloc(es, nc, "x0T", [128, KT * Wmax], F32)
        ssmb = _alloc(es, nc, "ssmb", [128, KT * Wmax], BF16)
        s1 = _alloc(es, nc, "s1", [128, KT * Wmax], F32)
        s1tail = _alloc(es, nc, "s1tail", [128, KT], F32)
        xkb = _alloc(es, nc, "xkb", [128, KT * Wmax], BF16)
        k2b = _alloc(es, nc, "k2b", [128, KT * Wmax], BF16)
        scrA = _alloc(es, nc, "scrA", [128, KT * Wmax], F32)
        srg = _alloc(es, nc, "srg", [128, Wmax], F32)
        s2t = _alloc(es, nc, "s2t", [128, KT * Wmax], F32)
        s2b = _alloc(es, nc, "s2b", [128, KT * Wmax], BF16)
        zet = [_alloc(es, nc, f"zet{i}", [128, Wmax], F32) for i in range(2)]
        ztail = _alloc(es, nc, "ztail", [128, ET * 3], F32)
        cacc = [_alloc(es, nc, f"cacc{i}", [128, Wmax], F32) for i in range(2)]
        xst = [_alloc(es, nc, f"xst{i}", [128, Wmax], F32) for i in range(2)]
        xcb = _alloc(es, nc, "xcb", [128, ET * Wmax], BF16)
        dtt = _alloc(es, nc, "dtt", [128, Wmax], F32)
        bct = _alloc(es, nc, "bct", [32, Wmax], F32)
        ps_a = _palloc(es, nc, "ps_a", [128, Wmax])
        ps_b = _palloc(es, nc, "ps_b", [128, Wmax])
        ps_tr = _palloc(es, nc, "ps_tr", [128, 128])
        ps_x = _palloc(es, nc, "ps_x", [32, Wmax])
        L = LNCtx(es, nc, Wmax)

        block = es.enter_context(nc.Block())
        p = Prog(nc)
        L.init_ops(p)

        cl = []
        for (dst, srcr) in ((lng_s, lng), (lnb_s, lnb), (csc_s, csc), (mur_s, mur),
                            (muk_s, muk), (bin_s, bin_), (cw_s, convw), (cb_s, convb),
                            (eyef_s, eyef), (hmask_s, hmask)):
            cl.append(p.dma("gpsimd", dst[:], srcr, tag="cst"))
        z0 = p.add("gpsimd", lambda g: g.memset(s1tail[:], 0.0), tag="z0")
        z1 = p.add("gpsimd", lambda g: g.memset(ztail[:], 0.0), tag="z1")
        L.const_ops = L.const_ops + cl

        unit_src = [Wctx, WkT, WvT, WrT] + [None] * 4 + [xpw]
        wu_readers = [[], []]
        useq = 0

        def load_unit(u_id, tag):
            nonlocal useq
            par = useq % 2
            useq += 1
            deps = wu_readers[par]
            ops = []
            if u_id < 4:
                srcw = unit_src[u_id]
                for k in range(KT):
                    ops.append(p.dma("gpsimd", wu[par][:, k * D:(k + 1) * D],
                                     srcw[k * 128:(k + 1) * 128, :],
                                     deps=deps if k == 0 else (), tag=f"{tag}k{k}"))
            elif u_id < 8:
                q = u_id - 4
                for k in range(KT):
                    ops.append(p.dma("gpsimd", wu[par][:, k * D:(k + 1) * D],
                                     Win[k * 128:(k + 1) * 128, q * D:(q + 1) * D],
                                     deps=deps if k == 0 else (), tag=f"{tag}k{k}"))
            else:
                for k in range(ET):
                    ops.append(p.dma("gpsimd", wu[par][:, k * 160:(k + 1) * 160],
                                     xpw[k * 128:(k + 1) * 128, :],
                                     deps=deps if k == 0 else (), tag=f"{tag}k{k}"))
            return par, ops

        prevc = {}

        def emit_chunk(ci, c0, W):
            tg = f"p2c{ci}_"
            store_ok = ci >= 1
            oc0 = c0 - HALO

            y0lds = []
            for k in range(KT):
                y0lds.append(p.dma(
                    "gpsimd", y0t[:, k * W:k * W + W],
                    y0_tok[k * 128:(k + 1) * 128, c0:c0 + W],
                    deps=prevc.get("y0t", []) if k == 0 else (), tag=f"{tg}y0ld{k}"))
            y0ld = y0lds[-1]
            ntt = (W + 127) // 128
            tr_evs = []
            sub_trs = []
            for sub in range(ntt):
                h = min(128, W - sub * 128)
                war = (sub_trs[sub - 2] if sub >= 2
                       else prevc.get("x0st%d" % (sub % 2), []))
                xld = p.dma("gpsimd", x0st[sub % 2][0:h, :],
                            x_tok[c0 + sub * 128: c0 + sub * 128 + h, :],
                            deps=war, tag=f"{tg}xld{sub}")
                evs = []
                trs = []
                for k in range(KT):
                    if evs:
                        guard = [evs[-1]]
                    elif tr_evs:
                        guard = [tr_evs[-1]]
                    else:
                        guard = prevc.get("ps_tr", [])
                    trm = p.add("tensor", lambda t, sub=sub, k=k, h=h: t.transpose(
                        ps_tr[:, 0:h], x0st[sub % 2][0:h, k * 128:(k + 1) * 128],
                        eyef_s[0:h, 0:h]),
                        deps=[xld] + guard + cl,
                        tag=f"{tg}tr{sub}_{k}")
                    ev = p.add("scalar", lambda s, sub=sub, k=k, h=h, W=W: s.copy(
                        x0T[:, k * W + sub * 128: k * W + sub * 128 + h], ps_tr[:, 0:h]),
                        deps=[trm] + (prevc.get("x0T", []) if sub == 0 and k == 0 else []),
                        tag=f"{tg}trev{sub}_{k}")
                    evs.append(ev)
                    trs.append(ev)
                tr_evs += evs
                sub_trs.append(trs)
            for par_ in range(min(2, ntt)):
                prevc["x0st%d" % ((ntt - 1 - par_) % 2)] = sub_trs[ntt - 1 - par_]
            prevc["ps_tr"] = [tr_evs[-1]]

            if stages < 1:
                return
            ln1 = emit_ln(p, L, y0t[:], KT, W, y0lds, f"{tg}ln1_", D,
                          gcol=lng_s, bcol=lnb_s, out_b16=ssmb)
            prevc["y0t"] = list(L.prev_readers)

            if stages < 2:
                return
            paru, wl = load_unit(0, f"{tg}wctx")
            ctx_adds = []
            for m in range(KT):
                ps = ps_a if m % 2 == 0 else ps_b
                mmlast = _mm_chain(
                    p, ps[:, 0:W],
                    [(wu[paru][:, k * D + m * 128: k * D + (m + 1) * 128],
                      ssmb[:, k * W:(k + 1) * W]) for k in range(KT)],
                    deps_first=wl + ln1 + prevc.get("ps%d" % (m % 2), []),
                    tag=f"{tg}ctx{m}")
                eva = p.add("scalar", lambda s, m=m, ps=ps: s.activation(
                    srg[:, 0:W], ps[:, 0:W], AF.Copy, scale=csc_s[:, 0:1]),
                    deps=[mmlast, tr_evs[-1]] + ([ctx_adds[-1]] if ctx_adds else []),
                    tag=f"{tg}ctxev{m}")
                add = p.add("vector", lambda v, m=m: v.tensor_add(
                    x0T[:, m * W:(m + 1) * W], x0T[:, m * W:(m + 1) * W], srg[:, 0:W]),
                    deps=[eva], tag=f"{tg}ctxadd{m}")
                ctx_adds.append(add)
                prevc["ps%d" % (m % 2)] = [eva]
            wu_readers[paru] = [mmlast]
            ln2 = emit_ln(p, L, x0T[:], KT, W, ctx_adds[-1], f"{tg}ln2_", D,
                          out_f32=s1)
            for a in ctx_adds[1:]:
                ln2[0].deps = ln2[0].deps + (a,)
            prevc["x0T"] = list(L.prev_readers)
            if ci == 0:
                mop = p.add("vector", lambda v: v.tensor_mul(
                    s1[:, 0:KT * W], s1[:, 0:KT * W], hmask_s[:, 0:KT * W]),
                    deps=ln2, tag=f"{tg}s1mask")
                ln2 = ln2 + [mop]

            def emit_mix(mucol_s, outb, dep, tagm):
                ops = []
                for k in range(KT):
                    hd = p.add("vector", lambda v, k=k: v.tensor_sub(
                        srg[:, 0:1], s1[:, k * W:k * W + 1], s1tail[:, k:k + 1]),
                        deps=[dep, z0] + prevc.get("s1tail", [])
                        + ([ops[-1]] if ops else []), tag=f"{tagm}h1_{k}")
                    hd2 = p.add("vector", lambda v, k=k: v.scalar_tensor_tensor(
                        outb[:, k * W:k * W + 1], srg[:, 0:1], mucol_s[:, k:k + 1],
                        s1tail[:, k:k + 1], op0=ALU.mult, op1=ALU.add),
                        deps=[hd], tag=f"{tagm}h2_{k}")
                    bd = p.add("vector", lambda v, k=k: v.tensor_sub(
                        srg[:, 1:W], s1[:, k * W + 1:k * W + W], s1[:, k * W:k * W + W - 1]),
                        deps=[hd2], tag=f"{tagm}b1_{k}")
                    bd2 = p.add("vector", lambda v, k=k: v.scalar_tensor_tensor(
                        outb[:, k * W + 1:k * W + W], srg[:, 1:W], mucol_s[:, k:k + 1],
                        s1[:, k * W:k * W + W - 1], op0=ALU.mult, op1=ALU.add),
                        deps=[bd], tag=f"{tagm}b2_{k}")
                    ops.append(bd2)
                return ops

            if stages < 3:
                return
            mixk = emit_mix(muk_s, xkb, ln2[-1], f"{tg}mk")
            paru, wl = load_unit(1, f"{tg}wk")
            krelu_ops = []
            for m in range(KT):
                ps = ps_a if m % 2 == 0 else ps_b
                mmlast = _mm_chain(
                    p, ps[:, 0:W],
                    [(wu[paru][:, k * D + m * 128: k * D + (m + 1) * 128],
                      xkb[:, k * W:(k + 1) * W]) for k in range(KT)],
                    deps_first=wl + [mixk[-1]] + prevc.get("ps%d" % (m % 2), []),
                    tag=f"{tg}kmm{m}")
                ev = p.add("scalar", lambda s, m=m, ps=ps: s.activation(
                    scrA[:, m * W:(m + 1) * W], ps[:, 0:W], AF.Relu),
                    deps=[mmlast] + prevc.get("scrA", []) if m == 0 else [mmlast],
                    tag=f"{tg}kev{m}")
                sqo = p.add("vector", lambda v, m=m: v.tensor_mul(
                    k2b[:, m * W:(m + 1) * W], scrA[:, m * W:(m + 1) * W],
                    scrA[:, m * W:(m + 1) * W]),
                    deps=[ev] + (prevc.get("k2b", []) if m == 0 else []),
                    tag=f"{tg}k2_{m}")
                krelu_ops.append(sqo)
                prevc["ps%d" % (m % 2)] = [ev]
            wu_readers[paru] = [mmlast]
            prevc["k2b"] = krelu_ops

            paru, wl = load_unit(2, f"{tg}wv")
            vout_ops = []
            for m in range(KT):
                ps = ps_a if m % 2 == 0 else ps_b
                mmlast = _mm_chain(
                    p, ps[:, 0:W],
                    [(wu[paru][:, k * D + m * 128: k * D + (m + 1) * 128],
                      k2b[:, k * W:(k + 1) * W]) for k in range(KT)],
                    deps_first=wl + krelu_ops + prevc.get("ps%d" % (m % 2), []),
                    tag=f"{tg}vmm{m}")
                ev = p.add("scalar", lambda s, m=m, ps=ps: s.copy(
                    scrA[:, m * W:(m + 1) * W], ps[:, 0:W]),
                    deps=[mmlast, krelu_ops[m]], tag=f"{tg}vev{m}")
                vout_ops.append(ev)
                prevc["ps%d" % (m % 2)] = [ev]
            wu_readers[paru] = [mmlast]

            mixr = emit_mix(mur_s, xkb, ln2[-1], f"{tg}mr")
            for mo in mixr:
                pass
            mixr[0].deps = mixr[0].deps + (wu_readers[(useq - 2) % 2][0],)
            paru, wl = load_unit(3, f"{tg}wr")
            ffn_ops = []
            for m in range(KT):
                ps = ps_a if m % 2 == 0 else ps_b
                mmlast = _mm_chain(
                    p, ps[:, 0:W],
                    [(wu[paru][:, k * D + m * 128: k * D + (m + 1) * 128],
                      xkb[:, k * W:(k + 1) * W]) for k in range(KT)],
                    deps_first=wl + [mixr[-1]] + prevc.get("ps%d" % (m % 2), []),
                    tag=f"{tg}rmm{m}")
                ev = p.add("scalar", lambda s, m=m, ps=ps: s.activation(
                    srg[:, 0:W], ps[:, 0:W], AF.Sigmoid),
                    deps=[mmlast] + ([ffn_ops[-1]] if ffn_ops else []), tag=f"{tg}rev{m}")
                fo = p.add("vector", lambda v, m=m: v.tensor_mul(
                    scrA[:, m * W:(m + 1) * W], scrA[:, m * W:(m + 1) * W], srg[:, 0:W]),
                    deps=[ev, vout_ops[m]], tag=f"{tg}ffn{m}")
                ffn_ops.append(fo)
                prevc["ps%d" % (m % 2)] = [ev]
            wu_readers[paru] = [mmlast]

            tailup = p.add("vector", lambda v: v.tensor_copy(
                s1tail[:], s1[:, 0:KT * W].rearrange("p (k w) -> p k w", k=KT)[:, :, W - 1:W]),
                deps=[mixr[-1], mixk[-1], z0], tag=f"{tg}tailup")
            prevc["s1tail"] = [tailup]
            r2_ops = []
            for k in range(KT):
                ro = p.add("vector", lambda v, k=k: v.tensor_add(
                    scrA[:, k * W:(k + 1) * W], scrA[:, k * W:(k + 1) * W],
                    s1[:, k * W:(k + 1) * W]),
                    deps=[ffn_ops[k], tailup], tag=f"{tg}r2_{k}")
                r2_ops.append(ro)
            prevc["s1"] = r2_ops + [tailup]
            ln3 = emit_ln(p, L, scrA[:], KT, W, r2_ops[-1], f"{tg}ln3_", D,
                          out_f32=s2t, out_b16=s2b)
            for a in r2_ops[1:]:
                ln3[0].deps = ln3[0].deps + (a,)
            prevc["scrA"] = list(L.prev_readers)
            if store_ok:
                s2sts = []
                for k in range(KT):
                    s2sts.append(p.dma(
                        "scalar", s2_out[k * 128:(k + 1) * 128, oc0:oc0 + W],
                        s2t[:, k * W:k * W + W], deps=ln3, tag=f"{tg}s2st{k}"))
                prevc["s2t"] = s2sts

            if stages < 4:
                return
            zlast = {}
            for q in range(4):
                paru, wl = load_unit(4 + q, f"{tg}win{q}")
                for m in range(KT):
                    et = q * 8 + m
                    ps = ps_a if m % 2 == 0 else ps_b
                    mmlast = _mm_chain(
                        p, ps[:, 0:W],
                        [(wu[paru][:, k * D + m * 128: k * D + (m + 1) * 128],
                          s2b[:, k * W:(k + 1) * W]) for k in range(KT)],
                        deps_first=wl + ln3 + prevc.get("ps%d" % (m % 2), []),
                        tag=f"{tg}zmm{et}")
                    if et < 16:
                        par2 = et % 2
                        ev = p.add("scalar", lambda s, et=et, ps=ps, par2=par2: s.activation(
                            zet[par2][:, 0:W], ps[:, 0:W], AF.Identity,
                            bias=bin_s[:, et:et + 1]),
                            deps=[mmlast] + prevc.get("zet%d" % par2, []),
                            tag=f"{tg}zev{et}")
                        prevc["ps%d" % (m % 2)] = [ev]
                        if ci == 0:
                            ev = p.add("vector", lambda v, par2=par2, W=W: v.tensor_mul(
                                zet[par2][:, 0:W], zet[par2][:, 0:W], hmask_s[:, 0:W]),
                                deps=[ev], tag=f"{tg}zmask{et}")
                        cv = []
                        ca = cacc[par2]
                        z = zet[par2]
                        tl3 = ztail[:, et * 3:et * 3 + 3]
                        dep0 = [ev, z1] + prevc.get("cacc%d" % par2, []) \
                            + prevc.get("ztail%d" % et, [])
                        if W > 3:
                            cv.append(p.add("vector", lambda v, ca=ca, tl3=tl3, et=et: v.tensor_scalar_mul(
                                ca[:, 0:3], tl3, cw_s[:, et * 4:et * 4 + 1]),
                                deps=dep0, tag=f"{tg}cv0h{et}"))
                            cv.append(p.add("vector", lambda v, ca=ca, z=z, et=et: v.tensor_scalar_mul(
                                ca[:, 3:W], z[:, 0:W - 3], cw_s[:, et * 4:et * 4 + 1]),
                                deps=dep0, tag=f"{tg}cv0b{et}"))
                            cv.append(p.add("vector", lambda v, ca=ca, tl3=tl3, et=et: v.scalar_tensor_tensor(
                                ca[:, 0:2], tl3[:, 1:3], cw_s[:, et * 4 + 1:et * 4 + 2],
                                ca[:, 0:2], op0=ALU.mult, op1=ALU.add),
                                deps=[cv[-1]], tag=f"{tg}cv1h{et}"))
                            cv.append(p.add("vector", lambda v, ca=ca, z=z, et=et: v.scalar_tensor_tensor(
                                ca[:, 2:W], z[:, 0:W - 2], cw_s[:, et * 4 + 1:et * 4 + 2],
                                ca[:, 2:W], op0=ALU.mult, op1=ALU.add),
                                deps=[cv[-1]], tag=f"{tg}cv1b{et}"))
                            cv.append(p.add("vector", lambda v, ca=ca, tl3=tl3, et=et: v.scalar_tensor_tensor(
                                ca[:, 0:1], tl3[:, 2:3], cw_s[:, et * 4 + 2:et * 4 + 3],
                                ca[:, 0:1], op0=ALU.mult, op1=ALU.add),
                                deps=[cv[-1]], tag=f"{tg}cv2h{et}"))
                            cv.append(p.add("vector", lambda v, ca=ca, z=z, et=et: v.scalar_tensor_tensor(
                                ca[:, 1:W], z[:, 0:W - 1], cw_s[:, et * 4 + 2:et * 4 + 3],
                                ca[:, 1:W], op0=ALU.mult, op1=ALU.add),
                                deps=[cv[-1]], tag=f"{tg}cv2b{et}"))
                            cv.append(p.add("vector", lambda v, ca=ca, z=z, et=et: v.scalar_tensor_tensor(
                                ca[:, 0:W], z[:, 0:W], cw_s[:, et * 4 + 3:et * 4 + 4],
                                ca[:, 0:W], op0=ALU.mult, op1=ALU.add),
                                deps=[cv[-1]], tag=f"{tg}cv3{et}"))
                        else:
                            cv.append(p.add("vector", lambda v, ca=ca, tl3=tl3, et=et: v.tensor_scalar_mul(
                                ca[:, 0:3], tl3, cw_s[:, et * 4:et * 4 + 1]),
                                deps=dep0, tag=f"{tg}c40h{et}"))
                            cv.append(p.add("vector", lambda v, ca=ca, z=z, et=et: v.tensor_scalar_mul(
                                ca[:, 3:4], z[:, 0:1], cw_s[:, et * 4:et * 4 + 1]),
                                deps=dep0 + [cv[-1]], tag=f"{tg}c40b{et}"))
                            cv.append(p.add("vector", lambda v, ca=ca, tl3=tl3, et=et: v.scalar_tensor_tensor(
                                ca[:, 0:2], tl3[:, 1:3], cw_s[:, et * 4 + 1:et * 4 + 2],
                                ca[:, 0:2], op0=ALU.mult, op1=ALU.add),
                                deps=[cv[-1]], tag=f"{tg}c41h{et}"))
                            cv.append(p.add("vector", lambda v, ca=ca, z=z, et=et: v.scalar_tensor_tensor(
                                ca[:, 2:4], z[:, 0:2], cw_s[:, et * 4 + 1:et * 4 + 2],
                                ca[:, 2:4], op0=ALU.mult, op1=ALU.add),
                                deps=[cv[-1]], tag=f"{tg}c41b{et}"))
                            cv.append(p.add("vector", lambda v, ca=ca, tl3=tl3, et=et: v.scalar_tensor_tensor(
                                ca[:, 0:1], tl3[:, 2:3], cw_s[:, et * 4 + 2:et * 4 + 3],
                                ca[:, 0:1], op0=ALU.mult, op1=ALU.add),
                                deps=[cv[-1]], tag=f"{tg}c42h{et}"))
                            cv.append(p.add("vector", lambda v, ca=ca, z=z, et=et: v.scalar_tensor_tensor(
                                ca[:, 1:4], z[:, 0:3], cw_s[:, et * 4 + 2:et * 4 + 3],
                                ca[:, 1:4], op0=ALU.mult, op1=ALU.add),
                                deps=[cv[-1]], tag=f"{tg}c42b{et}"))
                            cv.append(p.add("vector", lambda v, ca=ca, z=z, et=et: v.scalar_tensor_tensor(
                                ca[:, 0:4], z[:, 0:4], cw_s[:, et * 4 + 3:et * 4 + 4],
                                ca[:, 0:4], op0=ALU.mult, op1=ALU.add),
                                deps=[cv[-1]], tag=f"{tg}c43{et}"))
                        tup = p.add("vector", lambda v, z=z, tl3=tl3, W=W: v.tensor_copy(
                            tl3, z[:, W - 3:W]), deps=[cv[-1]], tag=f"{tg}ctu{et}")
                        prevc["zet%d" % par2] = [tup]
                        prevc["ztail%d" % et] = [tup]
                        sil = p.add("scalar", lambda s, et=et, par2=par2: s.activation(
                            xst[par2][:, 0:W], cacc[par2][:, 0:W], AF.Silu,
                            bias=cb_s[:, et:et + 1]),
                            deps=[cv[-1]] + prevc.get("xst%d" % par2, []),
                            tag=f"{tg}sil{et}")
                        prevc["cacc%d" % par2] = [sil]
                        cb16 = p.add("vector", lambda v, et=et, par2=par2: v.tensor_copy(
                            xcb[:, et * W:(et + 1) * W], xst[par2][:, 0:W]),
                            deps=[sil] + (prevc.get("xcb", []) if et == 0 else []),
                            tag=f"{tg}xcb{et}")
                        rd = [cb16]
                        if store_ok:
                            xcst = p.dma("scalar",
                                         xc_out[et * 128:(et + 1) * 128, oc0:oc0 + W],
                                         xst[par2][:, 0:W], deps=[sil],
                                         tag=f"{tg}xcst{et}")
                            rd.append(xcst)
                        prevc["xst%d" % par2] = rd
                        zlast[et] = cb16
                    else:
                        et2 = et - 16
                        par2 = et % 2
                        ev = p.add("scalar", lambda s, et=et, ps=ps, par2=par2, et2=et2: s.activation(
                            xst[par2][:, 0:W], ps[:, 0:W], AF.Silu,
                            bias=bin_s[:, 16 + et2:16 + et2 + 1]),
                            deps=[mmlast] + prevc.get("xst%d" % par2, []),
                            tag=f"{tg}xrev{et}")
                        prevc["ps%d" % (m % 2)] = [ev]
                        rd = [ev]
                        if store_ok:
                            xrst = p.dma("scalar",
                                         xr_out[et2 * 128:(et2 + 1) * 128, oc0:oc0 + W],
                                         xst[par2][:, 0:W], deps=[ev],
                                         tag=f"{tg}xrst{et}")
                            rd = [xrst]
                        prevc["xst%d" % par2] = rd
                wu_readers[paru] = [mmlast]
            prevc["xcb"] = []
            prevc["s2b"] = []

            if stages < 5:
                return
            paru, wl = load_unit(8, f"{tg}xpw")
            xc_deps = [zlast[et] for et in range(16)]
            mmd = _mm_chain(
                p, ps_a[:, 0:W],
                [(wu[paru][:, k * 160:k * 160 + 128], xcb[:, k * W:(k + 1) * W])
                 for k in range(ET)],
                deps_first=wl + xc_deps + prevc.get("ps0", []), tag=f"{tg}dtmm")
            evd = p.add("scalar", lambda s: s.copy(dtt[:, 0:W], ps_a[:, 0:W]),
                        deps=[mmd] + prevc.get("dtt", []), tag=f"{tg}dtev")
            prevc["ps0"] = [evd]
            mmb2 = _mm_chain(
                p, ps_x[:, 0:W],
                [(wu[paru][:, k * 160 + 128:(k + 1) * 160], xcb[:, k * W:(k + 1) * W])
                 for k in range(ET)],
                deps_first=xc_deps + prevc.get("ps_x", []), tag=f"{tg}bcm")
            evb = p.add("scalar", lambda s: s.copy(bct[:, 0:W], ps_x[:, 0:W]),
                        deps=[mmb2] + prevc.get("bct", []), tag=f"{tg}bcev")
            prevc["ps_x"] = [evb]
            wu_readers[paru] = [mmd, mmb2]
            prevc["xcb"] = [mmd, mmb2]
            if store_ok:
                d1 = p.dma("scalar", dt_out[:, oc0:oc0 + W], dtt[:, 0:W],
                           deps=[evd], tag=f"{tg}dtst")
                d2 = p.dma("scalar", bc_out[:, oc0:oc0 + W], bct[:, 0:W],
                           deps=[evb], tag=f"{tg}bcst")
                prevc["dtt"] = [d1]
                prevc["bct"] = [d2]

        for ci, (c0, W) in enumerate(zip(c0s, widths)):
            emit_chunk(ci, c0, W)

        p.emit(es, block)
    return nc


_P2 = None


def _phase2_nc():
    global _P2
    if _P2 is None:
        _P2 = build_phase2()
    return _P2


def run_phase2(inputs, y0_full):
    ET = 16
    x = np.ascontiguousarray(inputs["L0_attn_x"])
    eye = np.eye(128, dtype=np.float32)
    kcol = lambda vec: np.ascontiguousarray(np.asarray(vec, np.float32).reshape(-1, 128).T)
    in_maps = []
    for c in range(8):
        b, g = divmod(c, 4)
        t0 = g * TL
        xs = np.zeros((HALO + TL, D), np.float32)
        y0s = np.zeros((D, HALO + TL), np.float32)
        lo = max(0, t0 - HALO)
        xs[HALO - (t0 - lo):] = x[b, lo:t0 + TL]
        y0s[:, HALO - (t0 - lo):] = y0_full[b, lo:t0 + TL].T
        hm = np.ones((128, 32), np.float32)
        if g == 0:
            hm[:] = 0.0
        in_maps.append({
            "y0_tok": y0s, "x_tok": xs, "hmask": hm,
            "lng": kcol(inputs["L0_attn_ln_gamma"]),
            "lnb": kcol(inputs["L0_attn_ln_beta"]),
            "csc": np.full((128, 1), np.float32(inputs["L0_attn_context_scale"]), np.float32),
            "mur": kcol(inputs["L0_ffn_mu_r"]),
            "muk": kcol(inputs["L0_ffn_mu_k"]),
            "bin": np.ascontiguousarray(np.asarray(inputs["L1_attn_b_in"], np.float32).reshape(32, 128).T),
            "convw": np.ascontiguousarray(
                np.asarray(inputs["L1_attn_conv_w"], np.float32)[:, 0, :]
                .reshape(ET, 128, 4).transpose(1, 0, 2).reshape(128, ET * 4)),
            "convb": kcol(inputs["L1_attn_D_param"] * 0 + np.asarray(inputs["L1_attn_conv_b"], np.float32))
                if False else np.ascontiguousarray(np.asarray(inputs["L1_attn_conv_b"], np.float32).reshape(ET, 128).T),
            "eyef": eye,
            "Wctx": np.ascontiguousarray(inputs["L0_attn_W_ctx"]),
            "WrT": np.ascontiguousarray(inputs["L0_ffn_W_r"].T),
            "WkT": np.ascontiguousarray(inputs["L0_ffn_W_k"].T),
            "WvT": np.ascontiguousarray(inputs["L0_ffn_W_v"].T),
            "Win": np.ascontiguousarray(inputs["L1_attn_W_in"]),
            "xpw": np.ascontiguousarray(inputs["L1_attn_x_proj_w"]),
        })
    res = run_bass_kernel_spmd(_phase2_nc(), in_maps, core_ids=list(range(8)))
    out = {}
    for name, rows in (("s2", D), ("xr", E), ("xc", E), ("dt", 128), ("bc", 32)):
        arr = np.empty((B, rows, S), np.float32)
        for c in range(8):
            b, g = divmod(c, 4)
            arr[b][:, g * TL:(g + 1) * TL] = res.results[c][name]
        out[name] = arr
    return out



def build_phase3():
    nc = bass.Bass("TRN2", target_bir_lowering=False, detect_race_conditions=False)
    DT = DC1 // 128
    dtf = nc.dram_tensor("dtf", [128, S], F32, kind="ExternalInput").ap()
    dtpj = nc.dram_tensor("dtpj", [128, DC1], F32, kind="ExternalInput").ap()
    dtbj = nc.dram_tensor("dtbj", [128, DT], F32, kind="ExternalInput").ap()
    Alogj = nc.dram_tensor("Alogj", [DC1, N], F32, kind="ExternalInput").ap()
    Dj = nc.dram_tensor("Dj", [128, DT], F32, kind="ExternalInput").ap()
    xcs = nc.dram_tensor("xcs", [DC1, S], F32, kind="ExternalInput").ap()
    bcf = nc.dram_tensor("bcf", [2 * N, S], F32, kind="ExternalInput").ap()
    eye = nc.dram_tensor("eye", [128, 128], F32, kind="ExternalInput").ap()
    sel = nc.dram_tensor("sel", [2 * N, 2 * N * 128], F32, kind="ExternalInput").ap()
    y1 = nc.dram_tensor("y1", [DC1, S], F32, kind="ExternalOutput").ap()

    n_chunks = S // TC
    es = ExitStack()
    with es:
        dtf_s = _alloc(es, nc, "dtf_s", [128, S], BF16)
        dtp_s = _alloc(es, nc, "dtp_s", [128, DC1], BF16)
        dtb_s = _alloc(es, nc, "dtb_s", [128, DT], F32)
        dj_s = _alloc(es, nc, "dj_s", [128, DT], F32)
        acol = _alloc(es, nc, "acol", [128, DT * N], F32)
        eye_b = _alloc(es, nc, "eye_b", [128, 128], BF16)
        sel_sb = _alloc(es, nc, "sel_sb", [2 * N, 2 * N * 128], BF16)
        deltaT = [_alloc(es, nc, f"deltaT{i}", [128, DT * TC], F32) for i in range(2)]
        sp_tmp = _alloc(es, nc, "sp_tmp", [128, TC], F32)
        ut = [_alloc(es, nc, f"ut{i}", [128, DT * TC], F32) for i in range(2)]
        vT = [_alloc(es, nc, f"vT{i}", [128, DT * TC], BF16) for i in range(2)]
        bcal = [_alloc(es, nc, f"bcal{i}", [2 * N, TC], BF16) for i in range(2)]
        bbc = [_alloc(es, nc, f"bbc{i}", [128, TC], BF16) for i in range(2)]
        cbc = [_alloc(es, nc, f"cbc{i}", [128, TC], BF16) for i in range(2)]
        decay = [_alloc(es, nc, f"decay{i}", [128, TC], F32) for i in range(2)]
        btile = [_alloc(es, nc, f"btile{i}", [128, TC], BF16) for i in range(2)]
        htile = [_alloc(es, nc, f"htile{i}", [128, TC], F32) for i in range(2)]
        cm = [_alloc(es, nc, f"cm{i}", [128, TC], BF16) for i in range(2)]
        state = _alloc(es, nc, "state", [128, DT * N], F32)
        ytile = [_alloc(es, nc, f"ytile{i}", [128, DT * TC], F32) for i in range(2)]
        ps_mm = _palloc(es, nc, "ps_mm", [128, TC])
        ps_bb = [_palloc(es, nc, "ps_bb0", [128, TC])]
        ps_cb = [_palloc(es, nc, "ps_cb0", [128, TC])]
        ps_y = [_palloc(es, nc, f"ps_y{i}", [128, TC]) for i in range(DT)]

        block = es.enter_context(nc.Block())
        p = Prog(nc)

        ld0 = []
        ld0.append(p.dma("gpsimd", dtf_s[:], dtf, tag="dtf"))
        ld0.append(p.dma("gpsimd", dtp_s[:], dtpj, tag="dtp"))
        ld0.append(p.dma("gpsimd", dtb_s[:], dtbj, tag="dtb"))
        ld0.append(p.dma("gpsimd", dj_s[:], Dj, tag="dj"))
        ld0.append(p.dma("gpsimd", eye_b[:], eye, tag="eye"))
        ld0.append(p.dma("gpsimd", sel_sb[:], sel, tag="sel"))
        alog_ld = p.dma("gpsimd",
                        acol[:].rearrange("p (dt n) -> p dt n", dt=DT),
                        Alogj.rearrange("(dt p) n -> p dt n", dt=DT), tag="alog")
        st0 = p.add("gpsimd", lambda g: g.memset(state[:], 0.0), tag="st0")
        a_exp = p.add("scalar", lambda s: s.activation(acol[:], acol[:], AF.Exp),
                      deps=[alog_ld], tag="aexp")
        a_neg = p.add("vector", lambda v: v.tensor_scalar_mul(acol[:], acol[:], -1.0),
                      deps=[a_exp], tag="aneg")

        prev = {}
        state_prev = {c: st0 for c in range(DT * N)}

        def emit_chunk3(ci):
            nonlocal prev, state_prev
            par = ci % 2
            t0 = ci * TC
            tg = f"c{ci}_"
            uld = p.dma("gpsimd", ut[par][:].rearrange("p (dt t) -> p dt t", dt=DT),
                        xcs.rearrange("(dt p) s -> p dt s", dt=DT)[:, :, t0:t0 + TC],
                        deps=prev.get("ut_%d" % par, []), tag=f"{tg}uld")
            bld = p.dma("gpsimd", bcal[par][:], bcf[:, t0:t0 + TC],
                        deps=prev.get("bcal_%d" % par, []), tag=f"{tg}bld")
            d_ev = [None] * DT
            sp_war = prev.get("sp_tmp", [])
            for dt_i in range(DT):
                war_psmm = [d_ev[dt_i - 1]] if dt_i > 0 else prev.get("ps_mm", [])
                mm = p.add("tensor", lambda t, dt_i=dt_i, t0=t0: t.matmul(
                    ps_mm[:], dtp_s[:, dt_i * 128:(dt_i + 1) * 128],
                    dtf_s[:, t0:t0 + TC], start=True, stop=True),
                    deps=ld0 + war_psmm + ([a_neg] if ci == 0 else []),
                    tag=f"{tg}dmm{dt_i}")
                ex = p.add("scalar", lambda s, dt_i=dt_i: s.activation(
                    sp_tmp[:], ps_mm[:], AF.Exp, bias=dtb_s[:, dt_i:dt_i + 1]),
                    deps=[mm] + sp_war, tag=f"{tg}dex{dt_i}")
                sp_war = []
                ln = p.add("scalar", lambda s, dt_i=dt_i, par=par: s.activation(
                    deltaT[par][:, dt_i * TC:(dt_i + 1) * TC], sp_tmp[:], AF.Ln, bias=1.0),
                    deps=[ex] + (prev.get("deltaT_%d" % par, []) if dt_i == 0 else []),
                    tag=f"{tg}dln{dt_i}")
                d_ev[dt_i] = ln
            v_ops = []
            for dt_i in range(DT):
                v_ops.append(p.add("vector", lambda v, dt_i=dt_i, par=par: v.tensor_mul(
                    vT[par][:, dt_i * TC:(dt_i + 1) * TC],
                    deltaT[par][:, dt_i * TC:(dt_i + 1) * TC],
                    ut[par][:, dt_i * TC:(dt_i + 1) * TC]),
                    deps=[d_ev[dt_i], uld] + (prev.get("vT_%d" % par, []) if dt_i == 0 else []),
                    tag=f"{tg}v{dt_i}"))
            bufs = (acol, state, eye_b, sel_sb, bcal[par][:], deltaT[par][:], vT[par][:],
                    ps_bb, ps_cb, ps_y, bbc, cbc, decay, btile, htile, cm)
            ylast, readers, state_ops = scan_chunk(
                p, bufs, DT, d_ev[DT - 1], v_ops[-1], bld, dict(state_prev), prev, tg)
            state_prev = state_ops
            y_evs = []
            st_dmas = []
            for dt_i in range(DT):
                yev = p.add("scalar", lambda s, dt_i=dt_i, par=par: s.copy(
                    ytile[par][:, dt_i * TC:(dt_i + 1) * TC], ps_y[dt_i][:]),
                    deps=[ylast[dt_i]] + prev.get("ytile_%d" % par, []),
                    tag=f"{tg}yev{dt_i}")
                du = p.add("vector", lambda v, dt_i=dt_i, par=par: v.scalar_tensor_tensor(
                    ytile[par][:, dt_i * TC:(dt_i + 1) * TC],
                    ut[par][:, dt_i * TC:(dt_i + 1) * TC], dj_s[:, dt_i:dt_i + 1],
                    ytile[par][:, dt_i * TC:(dt_i + 1) * TC],
                    op0=ALU.mult, op1=ALU.add),
                    deps=[yev, uld], tag=f"{tg}du{dt_i}")
                y_evs.append(du)
                st_dma = p.dma("scalar",
                               y1[dt_i * 128:(dt_i + 1) * 128, t0:t0 + TC],
                               ytile[par][:, dt_i * TC:(dt_i + 1) * TC],
                               deps=[du], tag=f"{tg}store{dt_i}")
                st_dmas.append(st_dma)

            nxt = dict(prev)
            nxt.update(readers)
            nxt["ps_mm"] = [d_ev[DT - 1]]
            nxt["sp_tmp"] = [d_ev[DT - 1]]
            nxt["deltaT_%d" % par] = v_ops + readers["_delta"]
            nxt["bcal_%d" % par] = readers["_bcal"]
            nxt["vT_%d" % par] = readers["_v"]
            nxt["ut_%d" % par] = v_ops + y_evs
            nxt["ytile_%d" % par] = st_dmas
            for dt_i in range(DT):
                nxt["ps_y_%d" % dt_i] = [y_evs[dt_i]]
            prev = nxt

        for ci in range(n_chunks):
            emit_chunk3(ci)

        p.emit(es, block)
    return nc


_P3 = None


def _phase3_nc():
    global _P3
    if _P3 is None:
        _P3 = build_phase3()
    return _P3


def run_phase3(inputs, p2):
    DT = DC1 // 128
    eye = np.eye(128, dtype=np.float32)
    sel = np.zeros((2 * N, 2 * N * 128), dtype=np.float32)
    for i in range(2 * N):
        sel[i, i * 128:(i + 1) * 128] = 1.0
    dtp = np.ascontiguousarray(inputs["L1_attn_dt_proj_w"])
    dtb = np.asarray(inputs["L1_attn_dt_bias"], np.float32)
    Alog = np.ascontiguousarray(inputs["L1_attn_A_log"])
    Dp = np.asarray(inputs["L1_attn_D_param"], np.float32)
    in_maps = []
    for c in range(8):
        b, j = divmod(c, 4)
        sl = slice(DC1 * j, DC1 * (j + 1))
        in_maps.append({
            "dtf": np.ascontiguousarray(p2["dt"][b]),
            "dtpj": np.ascontiguousarray(dtp[:, sl]),
            "dtbj": np.ascontiguousarray(dtb[sl].reshape(DT, 128).T),
            "Alogj": np.ascontiguousarray(Alog[sl]),
            "Dj": np.ascontiguousarray(Dp[sl].reshape(DT, 128).T),
            "xcs": np.ascontiguousarray(p2["xc"][b][sl]),
            "bcf": np.ascontiguousarray(p2["bc"][b]),
            "eye": eye, "sel": sel,
        })
    res = run_bass_kernel_spmd(_phase3_nc(), in_maps, core_ids=list(range(8)))
    y1 = np.empty((B, E, S), np.float32)
    for c in range(8):
        b, j = divmod(c, 4)
        y1[b][DC1 * j:DC1 * (j + 1)] = res.results[c]["y1"]
    return y1



def _emit_colmax_update(p, L, acc_col, src_ap, deps, tag, first):
    m1 = p.add("vector", lambda v: v.reduce_max(
        L.sqs[:, 0:1], src_ap, axis=AX.X, apply_absolute_value=True),
        deps=deps + [L.maxrd] if L.maxrd else deps, tag=f"{tag}rm")
    L.maxrd = m1
    if first:
        m2 = p.add("vector", lambda v: v.tensor_copy(acc_col, L.sqs[:, 0:1]),
                   deps=[m1], tag=f"{tag}cp")
    else:
        m2 = p.add("vector", lambda v: v.tensor_max(acc_col, acc_col, L.sqs[:, 0:1]),
                   deps=[m1], tag=f"{tag}mx")
    L.maxrd = m2
    return m2


def _emit_pmax(p, es, nc, acc_col, out_ap, eyef_s, ps_tr, deps, tag):
    tr = p.add("tensor", lambda t: t.transpose(ps_tr[0:1, 0:128], acc_col, eyef_s[:]),
               deps=deps, tag=f"{tag}tr")
    rm = p.add("vector", lambda v: v.reduce_max(out_ap, ps_tr[0:1, 0:128], axis=AX.X),
               deps=[tr], tag=f"{tag}rm")
    return rm


def build_phase4a():
    nc = bass.Bass("TRN2", target_bir_lowering=False, detect_race_conditions=False)
    KT = 8
    ET = E // 128

    s2 = nc.dram_tensor("s2", [D, TL], F32, kind="ExternalInput").ap()
    y1t = nc.dram_tensor("y1t", [E, TL], F32, kind="ExternalInput").ap()
    xrg = nc.dram_tensor("xrg", [E, TL], F32, kind="ExternalInput").ap()
    WoutW = nc.dram_tensor("Wout", [E, D], F32, kind="ExternalInput").ap()
    bout = nc.dram_tensor("bout", [128, KT], F32, kind="ExternalInput").ap()
    wup = nc.dram_tensor("wup", [I3, D], F32, kind="ExternalInput").ap()
    wgate = nc.dram_tensor("wgate", [I3, D], F32, kind="ExternalInput").ap()
    wdown = nc.dram_tensor("wdown", [D, I3], F32, kind="ExternalInput").ap()
    eyef = nc.dram_tensor("eyef", [128, 128], F32, kind="ExternalInput").ap()

    s3_out = nc.dram_tensor("s3", [D, TL], F32, kind="ExternalOutput").ap()
    mx_out = nc.dram_tensor("mx", [1, 1], F32, kind="ExternalOutput").ap()
    wupT_q = nc.dram_tensor("wupT_q", [D, I3], BF16, kind="ExternalOutput").ap()
    wgateT_q = nc.dram_tensor("wgateT_q", [D, I3], BF16, kind="ExternalOutput").ap()
    wdownT_q = nc.dram_tensor("wdownT_q", [I3, D], BF16, kind="ExternalOutput").ap()

    TCW = 512
    es = ExitStack()
    with es:
        eyef_s = _alloc(es, nc, "eyef_s", [128, 128], F32)
        eye_b = _alloc(es, nc, "eye_b", [128, 128], BF16)
        bout_s = _alloc(es, nc, "bout_s", [128, KT], F32)
        wout_s = _alloc(es, nc, "wout_s", [128, ET * D], BF16)
        s2t = _alloc(es, nc, "s2t", [128, KT * TCW], F32)
        y1s = _alloc(es, nc, "y1s", [128, ET * TCW], F32)
        xrs = _alloc(es, nc, "xrs", [128, ET * TCW], F32)
        gt = _alloc(es, nc, "gt", [128, ET * TCW], BF16)
        s3t = _alloc(es, nc, "s3t", [128, KT * TCW], F32)
        mcol = _alloc(es, nc, "mcol", [128, 1], F32)
        mrow = _alloc(es, nc, "mrow", [1, 1], F32)
        wtile = _alloc(es, nc, "wtile", [128, I3 * 2], F32)
        sblk = _alloc(es, nc, "sblk", [128, 96 * 2], F32)
        invb = _alloc(es, nc, "invb", [128, 96 * 2], F32)
        dq = _alloc(es, nc, "dq", [128, I3 * 2], BF16)
        dqT = _alloc(es, nc, "dqT", [128, I3], BF16)
        ps_a = _palloc(es, nc, "ps_a", [128, TCW])
        ps_b = _palloc(es, nc, "ps_b", [128, TCW])
        ps_tr = _palloc(es, nc, "ps_tr", [128, 128])
        ps_trb = _palloc(es, nc, "ps_trb", [128, 128], BF16)
        L = LNCtx(es, nc, TCW)
        L.maxrd = None

        block = es.enter_context(nc.Block())
        p = Prog(nc)
        L.init_ops(p)
        cl = list(L.const_ops)
        cl.append(p.dma("gpsimd", eyef_s[:], eyef, tag="eye"))
        cl.append(p.dma("gpsimd", eye_b[:], eyef, tag="eyeb"))
        cl.append(p.dma("gpsimd", bout_s[:], bout, tag="bout"))
        wl = []
        for k in range(ET):
            wl.append(p.dma("gpsimd", wout_s[:, k * D:(k + 1) * D],
                            WoutW[k * 128:(k + 1) * 128, :], tag=f"wo{k}"))
        L.const_ops = cl

        prevc = {}
        mx_ops = []

        def emit_chunk4a(ci):
            nonlocal prevc, mx_ops
            t0 = ci * TCW
            tg = f"a{ci}_"
            lds = []
            for et in range(ET):
                lds.append(p.dma("gpsimd", y1s[:, et * TCW:(et + 1) * TCW],
                                 y1t[et * 128:(et + 1) * 128, t0:t0 + TCW],
                                 deps=prevc.get("y1s", []) if et == 0 else (),
                                 tag=f"{tg}yl{et}"))
                lds.append(p.dma("gpsimd", xrs[:, et * TCW:(et + 1) * TCW],
                                 xrg[et * 128:(et + 1) * 128, t0:t0 + TCW],
                                 deps=prevc.get("xrs", []) if et == 0 else (),
                                 tag=f"{tg}xl{et}"))
            s2ld = []
            for k in range(KT):
                s2ld.append(p.dma("gpsimd", s2t[:, k * TCW:(k + 1) * TCW],
                                  s2[k * 128:(k + 1) * 128, t0:t0 + TCW],
                                  deps=prevc.get("s2t", []) if k == 0 else (),
                                  tag=f"{tg}s2l{k}"))
            g_ops = []
            for et in range(ET):
                g_ops.append(p.add("vector", lambda v, et=et: v.tensor_mul(
                    gt[:, et * TCW:(et + 1) * TCW], y1s[:, et * TCW:(et + 1) * TCW],
                    xrs[:, et * TCW:(et + 1) * TCW]),
                    deps=lds + (prevc.get("gt", []) if et == 0 else []),
                    tag=f"{tg}g{et}"))
            radd = []
            for m in range(KT):
                ps = ps_a if m % 2 == 0 else ps_b
                mm = _mm_chain(p, ps[:, 0:TCW],
                               [(wout_s[:, k * D + m * 128: k * D + (m + 1) * 128],
                                 gt[:, k * TCW:(k + 1) * TCW]) for k in range(ET)],
                               deps_first=wl + g_ops + prevc.get("ps%d" % (m % 2), []),
                               tag=f"{tg}om{m}")
                ev = p.add("scalar", lambda s, m=m, ps=ps: s.activation(
                    s3t[:, m * TCW:(m + 1) * TCW], ps[:, 0:TCW], AF.Identity,
                    bias=bout_s[:, m:m + 1]),
                    deps=[mm] + (prevc.get("s3t", []) if m == 0 else []), tag=f"{tg}oe{m}")
                ad = p.add("vector", lambda v, m=m: v.tensor_add(
                    s3t[:, m * TCW:(m + 1) * TCW], s3t[:, m * TCW:(m + 1) * TCW],
                    s2t[:, m * TCW:(m + 1) * TCW]),
                    deps=[ev] + s2ld, tag=f"{tg}oa{m}")
                radd.append(ad)
                prevc["ps%d" % (m % 2)] = [ev]
            ln = emit_ln(p, L, s3t[:], KT, TCW, radd[-1], f"{tg}ln_", D, out_f32=s3t)
            for a in radd[1:]:
                ln[0].deps = ln[0].deps + (a,)
            sts = []
            for k in range(KT):
                sts.append(p.dma("scalar", s3_out[k * 128:(k + 1) * 128, t0:t0 + TCW],
                                 s3t[:, k * TCW:(k + 1) * TCW], deps=ln,
                                 tag=f"{tg}st{k}"))
                mx_ops.append(_emit_colmax_update(
                    p, L, mcol[:, 0:1], s3t[:, k * TCW:(k + 1) * TCW], list(ln),
                    f"{tg}mx{k}", first=(ci == 0 and k == 0)))
            prevc["s3t"] = sts + [mx_ops[-1]]
            prevc["gt"] = [mm]
            prevc["y1s"] = g_ops
            prevc["xrs"] = g_ops
            prevc["s2t"] = radd

        for ci in range(TL // TCW):
            emit_chunk4a(ci)
        pm = _emit_pmax(p, es, nc, mcol[:, 0:1], mrow[0:1, 0:1], eyef_s, ps_tr,
                        [mx_ops[-1]] + cl + list(prevc.get("ps_tr", [])), "pmax")
        p.dma("scalar", mx_out, mrow[0:1, 0:1], deps=[pm], tag="mxst")

        def dequant(wsrc, n_ot, n_i, outT, tagw):
            nblk = n_i // 32
            slots = {}
            for ot in range(n_ot):
                par = ot % 2
                g = prevc.get(f"dqw_{par}", [])
                if ot < 2:
                    g = g + prevc.get(f"dqw_{1 - par}", []) \
                        + prevc.get("dqb_0", []) + prevc.get("dqb_1", [])
                wld = p.dma("gpsimd", wtile[:, par * n_i:par * n_i + n_i],
                            wsrc[ot * 128:(ot + 1) * 128, :],
                            deps=g, tag=f"{tagw}wl{ot}")
                sb = sblk[:, par * 96: par * 96 + nblk]
                iv = invb[:, par * 96: par * 96 + nblk]
                r1 = p.add("vector", lambda v, par=par, nblk=nblk, n_i=n_i: v.reduce_max(
                    sblk[:, par * 96: par * 96 + nblk],
                    wtile[:, par * n_i:par * n_i + n_i].rearrange(
                        "p (b c) -> p b c", c=32),
                    axis=AX.X, apply_absolute_value=True),
                    deps=[wld] + g, tag=f"{tagw}s{ot}")
                r2 = p.add("vector", lambda v, sb=sb: v.tensor_scalar(
                    sb, sb, 1.0 / 7.0, 1e-12, op0=ALU.mult, op1=ALU.max),
                    deps=[r1], tag=f"{tagw}s2{ot}")
                r3 = p.add("vector", lambda v, iv=iv, sb=sb: v.reciprocal(iv, sb),
                           deps=[r2], tag=f"{tagw}inv{ot}")
                q1 = p.add("vector", lambda v, par=par, iv=iv, n_i=n_i, nblk=nblk: v.tensor_tensor(
                    wtile[:, par * n_i:par * n_i + n_i].rearrange("p (b c) -> p b c", c=32),
                    wtile[:, par * n_i:par * n_i + n_i].rearrange("p (b c) -> p b c", c=32),
                    iv.rearrange("p (b c) -> p b c", c=1).to_broadcast([128, nblk, 32]),
                    op=ALU.mult),
                    deps=[r3], tag=f"{tagw}q1{ot}")
                q2 = p.add("vector", lambda v, par=par, n_i=n_i: v.tensor_scalar(
                    wtile[:, par * n_i:par * n_i + n_i],
                    wtile[:, par * n_i:par * n_i + n_i],
                    MAGIC, MAGIC, op0=ALU.add, op1=ALU.subtract),
                    deps=[q1], tag=f"{tagw}q2{ot}")
                q3 = p.add("vector", lambda v, par=par, sb=sb, n_i=n_i, nblk=nblk: v.tensor_tensor(
                    dq[:, par * n_i:par * n_i + n_i].rearrange("p (b c) -> p b c", c=32),
                    wtile[:, par * n_i:par * n_i + n_i].rearrange("p (b c) -> p b c", c=32),
                    sb.rearrange("p (b c) -> p b c", c=1).to_broadcast([128, nblk, 32]),
                    op=ALU.mult),
                    deps=[q2] + prevc.get(f"dqb_{par}", [])
                    + (prevc.get(f"dqb_{1 - par}", []) if ot < 2 else []),
                    tag=f"{tagw}q3{ot}")
                trs = []
                for kb in range(n_i // 128):
                    tr = p.add("tensor", lambda t, par=par, kb=kb, n_i=n_i: t.matmul(
                        ps_trb[:], dq[:, par * n_i + kb * 128: par * n_i + (kb + 1) * 128],
                        eye_b[:], start=True, stop=True, is_transpose=True),
                        deps=[q3] + ([trs[-1]] if trs else prevc.get("pstr", [])),
                        tag=f"{tagw}tr{ot}_{kb}")
                    evo = p.add("scalar", lambda s, kb=kb: s.copy(
                        dqT[:, kb * 128:(kb + 1) * 128], ps_trb[:]),
                        deps=[tr] + (prevc.get("dqTg", []) if kb == 0 else []),
                        tag=f"{tagw}te{ot}_{kb}")
                    trs.append(evo)
                    prevc["pstr"] = [evo]
                std = p.dma("scalar",
                            outT.rearrange("(kb pp) o -> pp kb o", pp=128)[
                                :, 0:n_i // 128, ot * 128:(ot + 1) * 128],
                            dqT[:, 0:n_i].rearrange("p (kb c) -> p kb c", c=128),
                            deps=trs, tag=f"{tagw}st{ot}")
                prevc["dqTg"] = [std]
                prevc[f"dqw_{par}"] = [q3]
                prevc[f"dqb_{par}"] = trs

        dequant(wup, I3 // 128, D, wupT_q, "du")
        dequant(wgate, I3 // 128, D, wgateT_q, "dg")
        dequant(wdown, D // 128, I3, wdownT_q, "dd")

        p.emit(es, block)
    return nc



def _emit_scale_prep(p, L, nc, es, max_in_row, sxc, invc, ps_c, tag, qmax=32767.0):
    gm = p.add("vector", lambda v: v.reduce_max(L.rows[0:1, 0:1], max_in_row, axis=AX.X),
               deps=L.const_ops, tag=f"{tag}gm")
    sx = p.add("vector", lambda v: v.tensor_scalar(
        L.rows[0:1, 1:2], L.rows[0:1, 0:1], 1.0 / qmax, 1e-12,
        op0=ALU.mult, op1=ALU.max), deps=[gm], tag=f"{tag}sx")
    iv = p.add("vector", lambda v: v.reciprocal(L.rows[0:1, 2:3], L.rows[0:1, 1:2]),
               deps=[sx], tag=f"{tag}iv")
    mmb = p.add("tensor", lambda t: t.matmul(ps_c[:, 0:2], L.ones1f[:],
                                             L.rows[0:1, 1:3], start=True, stop=True),
                deps=[iv, sx], tag=f"{tag}bc")
    ev = p.add("scalar", lambda s: s.copy(sxc, ps_c[:, 0:1]), deps=[mmb], tag=f"{tag}e1")
    ev2 = p.add("scalar", lambda s: s.copy(invc, ps_c[:, 1:2]), deps=[mmb], tag=f"{tag}e2")
    return [ev, ev2]


def build_phase4b():
    nc = bass.Bass("TRN2", target_bir_lowering=False, detect_race_conditions=False)
    KT = 8
    IT = I3 // 128
    s3 = nc.dram_tensor("s3", [D, TL], F32, kind="ExternalInput").ap()
    mxall = nc.dram_tensor("mxall", [1, 8], F32, kind="ExternalInput").ap()
    wupT = nc.dram_tensor("wupT", [D, I3], BF16, kind="ExternalInput").ap()
    wgT = nc.dram_tensor("wgT", [D, I3], BF16, kind="ExternalInput").ap()
    eyef = nc.dram_tensor("eyef", [128, 128], F32, kind="ExternalInput").ap()
    up_o = nc.dram_tensor("up", [I3, TL], F32, kind="ExternalOutput").ap()
    g_o = nc.dram_tensor("gate", [I3, TL], F32, kind="ExternalOutput").ap()
    mup_o = nc.dram_tensor("mup", [1, 1], F32, kind="ExternalOutput").ap()
    mg_o = nc.dram_tensor("mg", [1, 1], F32, kind="ExternalOutput").ap()

    TCW = 512
    es = ExitStack()
    with es:
        eyef_s = _alloc(es, nc, "eyef_s", [128, 128], F32)
        wupb = _alloc(es, nc, "wupb", [128, KT * I3], BF16)
        wgb = _alloc(es, nc, "wgb", [128, KT * I3], BF16)
        s3t = _alloc(es, nc, "s3t", [128, KT * TCW], F32)
        xq = _alloc(es, nc, "xq", [128, KT * TCW], BF16)
        otile = [_alloc(es, nc, f"ot{i}", [128, TCW], F32) for i in range(2)]
        mcu = _alloc(es, nc, "mcu", [128, 1], F32)
        mcg = _alloc(es, nc, "mcg", [128, 1], F32)
        sxc = _alloc(es, nc, "sxc", [128, 1], F32)
        invc = _alloc(es, nc, "invc", [128, 1], F32)
        mr = _alloc(es, nc, "mr", [1, 2], F32)
        mxr = _alloc(es, nc, "mxr", [1, 8], F32)
        ps_a = _palloc(es, nc, "ps_a", [128, TCW])
        ps_b = _palloc(es, nc, "ps_b", [128, TCW])
        ps_tr = _palloc(es, nc, "ps_tr", [128, 128])
        ps_c = _palloc(es, nc, "ps_c", [128, 2])
        L = LNCtx(es, nc, TCW)
        L.maxrd = None

        block = es.enter_context(nc.Block())
        p = Prog(nc)
        L.init_ops(p)
        cl = list(L.const_ops)
        cl.append(p.dma("gpsimd", eyef_s[:], eyef, tag="eye"))
        cl.append(p.dma("gpsimd", mxr[0:1, :], mxall, tag="mx"))
        L.const_ops = cl
        wl = []
        for k in range(KT):
            wl.append(p.dma("gpsimd", wupb[:, k * I3:(k + 1) * I3],
                            wupT[k * 128:(k + 1) * 128, :], tag=f"wu{k}"))
            wl.append(p.dma("gpsimd", wgb[:, k * I3:(k + 1) * I3],
                            wgT[k * 128:(k + 1) * 128, :], tag=f"wg{k}"))
        sc_ev = _emit_scale_prep(p, L, nc, es, mxr[0:1, 0:8], sxc[:, 0:1],
                                 invc[:, 0:1], ps_c, "sp")

        prevc = {}
        mx_last = {"u": None, "g": None}

        def chunk4b(ci):
            t0 = ci * TCW
            tg = f"b{ci}_"
            lds = []
            for k in range(KT):
                lds.append(p.dma("gpsimd", s3t[:, k * TCW:(k + 1) * TCW],
                                 s3[k * 128:(k + 1) * 128, t0:t0 + TCW],
                                 deps=prevc.get("s3t", []) if k == 0 else (),
                                 tag=f"{tg}sl{k}"))
            xq_ops = []
            for k in range(KT):
                q1 = p.add("vector", lambda v, k=k: v.tensor_scalar(
                    L.sqs[:, 0:TCW], s3t[:, k * TCW:(k + 1) * TCW],
                    invc[:, 0:1], MAGIC, op0=ALU.mult, op1=ALU.add),
                    deps=lds + sc_ev + (prevc.get("xq", []) if k == 0 else []),
                    tag=f"{tg}q1{k}")
                q2 = p.add("vector", lambda v, k=k: v.tensor_scalar(
                    xq[:, k * TCW:(k + 1) * TCW], L.sqs[:, 0:TCW],
                    MAGIC, None, op0=ALU.subtract),
                    deps=[q1], tag=f"{tg}q2{k}")
                xq_ops.append(q2)
            prevc["s3t"] = xq_ops[-1:]
            prevc["xq"] = []
            return xq_ops

        del chunk4b

        def chunk4b(ci):
            t0 = ci * TCW
            tg = f"b{ci}_"
            lds = []
            for k in range(KT):
                lds.append(p.dma("gpsimd", s3t[:, k * TCW:(k + 1) * TCW],
                                 s3[k * 128:(k + 1) * 128, t0:t0 + TCW],
                                 deps=prevc.get("s3t", []) if k == 0 else (),
                                 tag=f"{tg}sl{k}"))
            xq_ops = []
            for k in range(KT):
                q1 = p.add("vector", lambda v, k=k: v.tensor_scalar(
                    L.sqs[:, 0:TCW], s3t[:, k * TCW:(k + 1) * TCW],
                    invc[:, 0:1], MAGIC, op0=ALU.mult, op1=ALU.add),
                    deps=lds + sc_ev + (prevc.get("xqw", []) if k == 0 else []),
                    tag=f"{tg}q1{k}")
                q2 = p.add("vector", lambda v, k=k: v.tensor_scalar(
                    xq[:, k * TCW:(k + 1) * TCW], L.sqs[:, 0:TCW],
                    MAGIC, sxc[:, 0:1], op0=ALU.subtract, op1=ALU.mult),
                    deps=[q1], tag=f"{tg}q2{k}")
                xq_ops.append(q2)
            prevc["s3t"] = [xq_ops[-1]]
            prevc["xqw"] = []

            for which, wb, oo, mc in (("u", wupb, up_o, mcu), ("g", wgb, g_o, mcg)):
                for m in range(IT):
                    par = m % 2
                    ps = ps_a if par == 0 else ps_b
                    mm = _mm_chain(p, ps[:, 0:TCW],
                                   [(wb[:, k * I3 + m * 128: k * I3 + (m + 1) * 128],
                                     xq[:, k * TCW:(k + 1) * TCW]) for k in range(KT)],
                                   deps_first=wl + xq_ops + prevc.get("ps%d" % par, []),
                                   tag=f"{tg}{which}m{m}")
                    ev = p.add("scalar", lambda s, par=par, ps=ps: s.copy(
                        otile[par][:, 0:TCW], ps[:, 0:TCW]),
                        deps=[mm] + prevc.get("ot%d" % par, []), tag=f"{tg}{which}e{m}")
                    prevc["ps%d" % par] = [ev]
                    std = p.dma("scalar", oo[m * 128:(m + 1) * 128, t0:t0 + TCW],
                                otile[par][:, 0:TCW], deps=[ev], tag=f"{tg}{which}s{m}")
                    mx = _emit_colmax_update(p, L, mc[:, 0:1], otile[par][:, 0:TCW],
                                             [ev], f"{tg}{which}x{m}",
                                             first=(ci == 0 and m == 0))
                    mx_last[which] = mx
                    prevc["ot%d" % par] = [std, mx]
            prevc["xqw"] = [mm]

        for ci in range(TL // TCW):
            chunk4b(ci)
        pm1 = _emit_pmax(p, es, nc, mcu[:, 0:1], mr[0:1, 0:1], eyef_s, ps_tr,
                         [mx_last["u"]] + cl, "pu")
        pm2 = _emit_pmax(p, es, nc, mcg[:, 0:1], mr[0:1, 1:2], eyef_s, ps_tr,
                         [mx_last["g"], pm1], "pg")
        p.dma("scalar", mup_o, mr[0:1, 0:1], deps=[pm1], tag="m1st")
        p.dma("scalar", mg_o, mr[0:1, 1:2], deps=[pm2], tag="m2st")
        p.emit(es, block)
    return nc


def _build_elemwise_phase(name, n_in2, with_sig):
    nc = bass.Bass("TRN2", target_bir_lowering=False, detect_race_conditions=False)
    IT = I3 // 128
    a_in = nc.dram_tensor("a", [I3, TL], F32, kind="ExternalInput").ap()
    b_in = nc.dram_tensor("b", [I3, TL], F32, kind="ExternalInput").ap() if n_in2 else None
    ma_all = nc.dram_tensor("ma", [1, 8], F32, kind="ExternalInput").ap()
    mb_all = nc.dram_tensor("mb", [1, 8], F32, kind="ExternalInput").ap() if n_in2 else None
    eyef = nc.dram_tensor("eyef", [128, 128], F32, kind="ExternalInput").ap()
    o_out = nc.dram_tensor("o", [I3, TL], F32, kind="ExternalOutput").ap()
    mo_out = nc.dram_tensor("mo", [1, 1], F32, kind="ExternalOutput").ap()

    TCW = 512
    es = ExitStack()
    with es:
        eyef_s = _alloc(es, nc, "eyef_s", [128, 128], F32)
        at = [_alloc(es, nc, f"at{i}", [128, TCW], F32) for i in range(2)]
        bt = [_alloc(es, nc, f"bt{i}", [128, TCW], F32) for i in range(2)]
        sg = _alloc(es, nc, "sg", [128, TCW], F32)
        ot = [_alloc(es, nc, f"ot{i}", [128, TCW], F32) for i in range(2)]
        mco = _alloc(es, nc, "mco", [128, 1], F32)
        sca = _alloc(es, nc, "sca", [128, 2], F32)
        scb = _alloc(es, nc, "scb", [128, 2], F32)
        mr = _alloc(es, nc, "mr", [1, 1], F32)
        mxr = _alloc(es, nc, "mxr", [1, 16], F32)
        ps_tr = _palloc(es, nc, "ps_tr", [128, 128])
        ps_c = _palloc(es, nc, "ps_c", [128, 2])
        L = LNCtx(es, nc, TCW)
        L.maxrd = None

        block = es.enter_context(nc.Block())
        p = Prog(nc)
        L.init_ops(p)
        cl = list(L.const_ops)
        cl.append(p.dma("gpsimd", eyef_s[:], eyef, tag="eye"))
        cl.append(p.dma("gpsimd", mxr[0:1, 0:8], ma_all, tag="ma"))
        if n_in2:
            cl.append(p.dma("gpsimd", mxr[0:1, 8:16], mb_all, tag="mb"))
        L.const_ops = cl
        sc_ev = _emit_scale_prep(p, L, nc, es, mxr[0:1, 0:8], sca[:, 0:1],
                                 sca[:, 1:2], ps_c, "sa")
        if n_in2:
            sc_ev += _emit_scale_prep(p, L, nc, es, mxr[0:1, 8:16], scb[:, 0:1],
                                      scb[:, 1:2], ps_c, "sb")

        prevc = {}
        mx_last = [None]

        def tile4(ci, it):
            par = it % 2
            t0 = ci * TCW
            tg = f"t{ci}_{it}_"
            la = p.dma("gpsimd", at[par][:], a_in[it * 128:(it + 1) * 128, t0:t0 + TCW],
                       deps=prevc.get("at%d" % par, []), tag=f"{tg}la")
            deps0 = [la] + sc_ev
            q1 = p.add("vector", lambda v, par=par: v.tensor_scalar(
                at[par][:], at[par][:], sca[:, 1:2], MAGIC, op0=ALU.mult, op1=ALU.add),
                deps=deps0, tag=f"{tg}q1")
            q2 = p.add("vector", lambda v, par=par: v.tensor_scalar(
                at[par][:], at[par][:], MAGIC, sca[:, 0:1],
                op0=ALU.subtract, op1=ALU.mult), deps=[q1], tag=f"{tg}q2")
            if with_sig:
                s1 = p.add("scalar", lambda s, par=par: s.activation(
                    sg[:], at[par][:], AF.Sigmoid),
                    deps=[q2] + prevc.get("sg", []), tag=f"{tg}sig")
                s2 = p.add("vector", lambda v: v.tensor_scalar(
                    sg[:], sg[:], 65536.0, MAGIC, op0=ALU.mult, op1=ALU.add),
                    deps=[s1], tag=f"{tg}s2")
                s3_ = p.add("vector", lambda v: v.tensor_scalar(
                    sg[:], sg[:], MAGIC, 65535.0, op0=ALU.subtract, op1=ALU.min),
                    deps=[s2], tag=f"{tg}s3")
                s4 = p.add("vector", lambda v: v.tensor_scalar(
                    sg[:], sg[:], 1.0 / 65536.0, None, op0=ALU.mult),
                    deps=[s3_], tag=f"{tg}s4")
                oo = p.add("vector", lambda v, par=par: v.tensor_mul(
                    ot[par][:], at[par][:], sg[:]),
                    deps=[s4, q2] + prevc.get("ot%d" % par, []), tag=f"{tg}o")
                prevc["sg"] = [oo]
            else:
                lb = p.dma("gpsimd", bt[par][:], b_in[it * 128:(it + 1) * 128, t0:t0 + TCW],
                           deps=prevc.get("bt%d" % par, []), tag=f"{tg}lb")
                r1 = p.add("vector", lambda v, par=par: v.tensor_scalar(
                    bt[par][:], bt[par][:], scb[:, 1:2], MAGIC, op0=ALU.mult, op1=ALU.add),
                    deps=[lb] + sc_ev, tag=f"{tg}r1")
                r2 = p.add("vector", lambda v, par=par: v.tensor_scalar(
                    bt[par][:], bt[par][:], MAGIC, scb[:, 0:1],
                    op0=ALU.subtract, op1=ALU.mult), deps=[r1], tag=f"{tg}r2")
                oo = p.add("vector", lambda v, par=par: v.tensor_mul(
                    ot[par][:], at[par][:], bt[par][:]),
                    deps=[q2, r2] + prevc.get("ot%d" % par, []), tag=f"{tg}o")
                prevc["bt%d" % par] = [oo]
            st = p.dma("scalar", o_out[it * 128:(it + 1) * 128, t0:t0 + TCW],
                       ot[par][:], deps=[oo], tag=f"{tg}st")
            mx = _emit_colmax_update(p, L, mco[:, 0:1], ot[par][:], [oo],
                                     f"{tg}mx", first=(ci == 0 and it == 0))
            mx_last[0] = mx
            prevc["ot%d" % par] = [st, mx]
            prevc["at%d" % par] = [oo]

        for ci in range(TL // TCW):
            for it in range(IT):
                tile4(ci, it)
        pm = _emit_pmax(p, es, nc, mco[:, 0:1], mr[0:1, 0:1], eyef_s, ps_tr,
                        [mx_last[0]] + cl, "pm")
        p.dma("scalar", mo_out, mr[0:1, 0:1], deps=[pm], tag="most")
        p.emit(es, block)
    return nc


def build_phase4c():
    return _build_elemwise_phase("c", False, True)


def build_phase4d():
    return _build_elemwise_phase("d", True, False)


def build_phase4e():
    nc = bass.Bass("TRN2", target_bir_lowering=False, detect_race_conditions=False)
    KT = 8
    IT = I3 // 128
    o_in = nc.dram_tensor("o", [I3, TL], F32, kind="ExternalInput").ap()
    mo_all = nc.dram_tensor("mo", [1, 8], F32, kind="ExternalInput").ap()
    wdT = nc.dram_tensor("wdT", [I3, D], BF16, kind="ExternalInput").ap()
    s3 = nc.dram_tensor("s3", [D, TL], F32, kind="ExternalInput").ap()
    eyef = nc.dram_tensor("eyef", [128, 128], F32, kind="ExternalInput").ap()
    out_t = nc.dram_tensor("out", [TL, D], F32, kind="ExternalOutput").ap()

    TCW = 512
    es = ExitStack()
    with es:
        eyef_s = _alloc(es, nc, "eyef_s", [128, 128], F32)
        wdb = _alloc(es, nc, "wdb", [128, IT * D], BF16)
        oq = _alloc(es, nc, "oq", [128, IT * TCW], BF16)
        s3t = _alloc(es, nc, "s3t", [128, KT * TCW], F32)
        ft = _alloc(es, nc, "ft", [128, KT * TCW], F32)
        sca = _alloc(es, nc, "sca", [128, 2], F32)
        mxr = _alloc(es, nc, "mxr", [1, 8], F32)
        otm = _alloc(es, nc, "otm", [128, D], F32)
        ps_a = _palloc(es, nc, "ps_a", [128, TCW])
        ps_b = _palloc(es, nc, "ps_b", [128, TCW])
        ps_tr = _palloc(es, nc, "ps_tr", [128, 128])
        ps_c = _palloc(es, nc, "ps_c", [128, 2])
        L = LNCtx(es, nc, TCW)
        L.maxrd = None

        block = es.enter_context(nc.Block())
        p = Prog(nc)
        L.init_ops(p)
        cl = list(L.const_ops)
        cl.append(p.dma("gpsimd", eyef_s[:], eyef, tag="eye"))
        cl.append(p.dma("gpsimd", mxr[0:1, :], mo_all, tag="mo"))
        L.const_ops = cl
        wl = []
        for k in range(IT):
            wl.append(p.dma("gpsimd", wdb[:, k * D:(k + 1) * D],
                            wdT[k * 128:(k + 1) * 128, :], tag=f"wd{k}"))
        sc_ev = _emit_scale_prep(p, L, nc, es, mxr[0:1, 0:8], sca[:, 0:1],
                                 sca[:, 1:2], ps_c, "so")

        prevc = {}

        def chunk4e(ci):
            t0 = ci * TCW
            tg = f"e{ci}_"
            oq_ops = []
            for it in range(IT):
                lo = p.dma("gpsimd", L.sqs[:, 0:TCW],
                           o_in[it * 128:(it + 1) * 128, t0:t0 + TCW],
                           deps=prevc.get("osq", []) if it == 0 else prevc.get("osq_i", []),
                           tag=f"{tg}lo{it}")
                q1 = p.add("vector", lambda v: v.tensor_scalar(
                    L.sqs[:, 0:TCW], L.sqs[:, 0:TCW], sca[:, 1:2], MAGIC,
                    op0=ALU.mult, op1=ALU.add),
                    deps=[lo] + sc_ev, tag=f"{tg}q1{it}")
                q2 = p.add("vector", lambda v, it=it: v.tensor_scalar(
                    oq[:, it * TCW:(it + 1) * TCW], L.sqs[:, 0:TCW], MAGIC, sca[:, 0:1],
                    op0=ALU.subtract, op1=ALU.mult),
                    deps=[q1] + (prevc.get("oq", []) if it == 0 else []), tag=f"{tg}q2{it}")
                oq_ops.append(q2)
                prevc["osq_i"] = [q2]
            prevc["osq"] = [oq_ops[-1]]
            s3ld = []
            for k in range(KT):
                s3ld.append(p.dma("gpsimd", s3t[:, k * TCW:(k + 1) * TCW],
                                  s3[k * 128:(k + 1) * 128, t0:t0 + TCW],
                                  deps=prevc.get("s3t", []) if k == 0 else (),
                                  tag=f"{tg}s3l{k}"))
            radd = []
            for m in range(KT):
                par = m % 2
                ps = ps_a if par == 0 else ps_b
                mm = _mm_chain(p, ps[:, 0:TCW],
                               [(wdb[:, k * D + m * 128: k * D + (m + 1) * 128],
                                 oq[:, k * TCW:(k + 1) * TCW]) for k in range(IT)],
                               deps_first=wl + oq_ops + prevc.get("ps%d" % par, []),
                               tag=f"{tg}dm{m}")
                ev = p.add("scalar", lambda s, m=m, ps=ps: s.copy(
                    ft[:, m * TCW:(m + 1) * TCW], ps[:, 0:TCW]),
                    deps=[mm] + (prevc.get("ft", []) if m == 0 else []), tag=f"{tg}de{m}")
                ad = p.add("vector", lambda v, m=m: v.tensor_add(
                    ft[:, m * TCW:(m + 1) * TCW], ft[:, m * TCW:(m + 1) * TCW],
                    s3t[:, m * TCW:(m + 1) * TCW]),
                    deps=[ev] + s3ld, tag=f"{tg}da{m}")
                radd.append(ad)
                prevc["ps%d" % par] = [ev]
            ln = emit_ln(p, L, ft[:], KT, TCW, radd[-1], f"{tg}ln_", D, out_f32=ft)
            for a in radd[1:]:
                ln[0].deps = ln[0].deps + (a,)
            prevc["s3t"] = list(ln)
            prevc["osq"] = list(ln) + [oq_ops[-1]]
            prevc["osq_i"] = list(ln)
            tb_prev = prevc.get("tb", [])
            for tb in range(TCW // 128):
                evs = []
                for k in range(KT):
                    tr = p.add("tensor", lambda t, tb=tb, k=k: t.transpose(
                        ps_tr[:], ft[:, k * TCW + tb * 128: k * TCW + (tb + 1) * 128],
                        eyef_s[:]),
                        deps=list(ln) + (tb_prev if tb == 0 and k == 0 else [])
                        + ([evs[-1]] if evs else prevc.get("pstr", [])),
                        tag=f"{tg}tr{tb}_{k}")
                    ev = p.add("scalar", lambda s, k=k: s.copy(
                        otm[:, k * 128:(k + 1) * 128], ps_tr[:]),
                        deps=[tr] + prevc.get("otm", []),
                        tag=f"{tg}te{tb}_{k}")
                    evs.append(ev)
                    prevc["pstr"] = [ev]
                st = p.dma("scalar", out_t[t0 + tb * 128: t0 + (tb + 1) * 128, :],
                           otm[:], deps=evs, tag=f"{tg}ot{tb}")
                prevc["otm"] = [st]
            prevc["tb"] = [st]
            prevc["ft"] = list(ln)
            prevc["oq"] = [mm]

        for ci in range(TL // TCW):
            chunk4e(ci)
        p.emit(es, block)
    return nc


_P4 = {}


def _phase4_nc(name):
    if name not in _P4:
        _P4[name] = {"a": build_phase4a, "b": build_phase4b, "c": build_phase4c,
                     "d": build_phase4d, "e": build_phase4e}[name]()
    return _P4[name]


def run_phase4(inputs, s2_arr, xr_arr, y1_arr):
    eye = np.eye(128, dtype=np.float32)
    kcol = lambda vec: np.ascontiguousarray(np.asarray(vec, np.float32).reshape(-1, 128).T)

    def slc(arr, c):
        b, g = divmod(c, 4)
        return np.ascontiguousarray(arr[b][:, g * TL:(g + 1) * TL])

    in_maps = []
    for c in range(8):
        in_maps.append({
            "s2": slc(s2_arr, c), "y1t": slc(y1_arr, c), "xrg": slc(xr_arr, c),
            "Wout": np.ascontiguousarray(inputs["L1_attn_W_out"]),
            "bout": kcol(inputs["L1_attn_b_out"]),
            "wup": np.ascontiguousarray(inputs["L1_ffn_w_up"]),
            "wgate": np.ascontiguousarray(inputs["L1_ffn_w_gate"]),
            "wdown": np.ascontiguousarray(inputs["L1_ffn_w_down"]),
            "eyef": eye,
        })
    ra = run_bass_kernel_spmd(_phase4_nc("a"), in_maps, core_ids=list(range(8)))
    s3s = [ra.results[c]["s3"] for c in range(8)]
    mxall = np.ascontiguousarray(
        np.concatenate([ra.results[c]["mx"] for c in range(8)], axis=1))
    in_maps = [{"s3": s3s[c], "mxall": mxall,
                "wupT": ra.results[c]["wupT_q"], "wgT": ra.results[c]["wgateT_q"],
                "eyef": eye} for c in range(8)]
    rb = run_bass_kernel_spmd(_phase4_nc("b"), in_maps, core_ids=list(range(8)))
    mup = np.ascontiguousarray(np.concatenate([rb.results[c]["mup"] for c in range(8)], 1))
    mg = np.ascontiguousarray(np.concatenate([rb.results[c]["mg"] for c in range(8)], 1))
    in_maps = [{"a": rb.results[c]["gate"], "ma": mg, "eyef": eye} for c in range(8)]
    rc = run_bass_kernel_spmd(_phase4_nc("c"), in_maps, core_ids=list(range(8)))
    mg2 = np.ascontiguousarray(np.concatenate([rc.results[c]["mo"] for c in range(8)], 1))
    in_maps = [{"a": rc.results[c]["o"], "b": rb.results[c]["up"],
                "ma": mg2, "mb": mup, "eyef": eye} for c in range(8)]
    rd = run_bass_kernel_spmd(_phase4_nc("d"), in_maps, core_ids=list(range(8)))
    mo = np.ascontiguousarray(np.concatenate([rd.results[c]["mo"] for c in range(8)], 1))
    in_maps = [{"o": rd.results[c]["o"], "mo": mo, "wdT": ra.results[c]["wdownT_q"],
                "s3": s3s[c], "eyef": eye} for c in range(8)]
    re_ = run_bass_kernel_spmd(_phase4_nc("e"), in_maps, core_ids=list(range(8)))
    out = np.empty((B, S, D), np.float32)
    for c in range(8):
        b, g = divmod(c, 4)
        out[b][g * TL:(g + 1) * TL] = re_.results[c]["out"]
    return out


def kernel(**inputs):
    y0 = run_phase1(inputs)
    p2 = run_phase2(inputs, y0)
    y1arr = run_phase3(inputs, p2)
    out = run_phase4(inputs, p2["s2"], p2["xr"], y1arr)
    return out
